# version 03
# baseline (speedup 1.0000x reference)
import numpy as np
import concourse.bass as bass
import concourse.mybir as mybir
from concourse.bass_utils import run_bass_kernel_spmd

F32 = mybir.dt.float32
BF16 = mybir.dt.bfloat16
AF = mybir.ActivationFunctionType
ALU = mybir.AluOpType
AX = mybir.AxisListType

S, D, DL = 2048, 4096, 512
NT, ND, NL, NH = 16, 32, 4, 32
OS = 512
NEPS = 1e-5
QK_SCALE = float(1.0 / np.sqrt(128.0))

ENGINES = ("tensor", "vector", "scalar", "sync", "gpsimd")
DMA_SEMS = ("sin", "sout")


def AP(h, off, dims):
    return bass.AP(h, off, [list(d) for d in dims])


class Plan:
    def __init__(self):
        self.ops = {e: [] for e in ENGINES}
        self.cnt = {"sin": 0, "sout": 0, "st": 0, "sv": 0, "ss": 0}

    def now(self, eng, sem):
        return (eng, sem, self.cnt[sem])

    def emit(self, eng, fn, incs=(), waits=(), fwaits=()):
        lst = self.ops[eng]
        merged = {}
        for w in waits:
            if w is None:
                continue
            weng, wsem, wval = w
            if weng == eng and wsem not in DMA_SEMS:
                continue
            if wval <= 0:
                continue
            merged[wsem] = max(merged.get(wsem, 0), wval)
        for w in fwaits:
            if w is None:
                continue
            weng, wsem, wval = w
            if wval <= 0:
                continue
            merged[wsem] = max(merged.get(wsem, 0), wval)
        for wsem, wval in merged.items():
            lst.append(("w", wsem, wval))
        lst.append(("i", fn, tuple(incs)))
        ev = None
        for (s, k) in incs:
            self.cnt[s] += k
            ev = (eng, s, self.cnt[s])
        return ev

    def din(self, dst, src, waits=()):
        return self.emit("sync",
                         lambda e, d=dst, s=src: e.dma_start(out=d, in_=s),
                         [("sin", 16)], waits)

    def dout(self, dst, src, waits=()):
        return self.emit("gpsimd",
                         lambda e, d=dst, s=src: e.dma_start(out=d, in_=s),
                         [("sout", 16)], waits)

    def mm(self, out, lhsT, rhs, start, stop, inc=False, waits=()):
        return self.emit(
            "tensor",
            lambda e, o=out, l=lhsT, r=rhs, a=start, b=stop:
                e.matmul(o, l, r, start=a, stop=b),
            [("st", 1)] if inc else [], waits)

    def tr(self, out, in_, ident, inc=False, waits=(), start=True, stop=True):
        return self.emit(
            "tensor",
            lambda e, o=out, i=in_, d=ident, a=start, b=stop:
                e.matmul(o, i, d, start=a, stop=b, is_transpose=True),
            [("st", 1)] if inc else [], waits)

    def v_tt(self, op, out, a, b, waits=()):
        return self.emit(
            "vector",
            lambda e, o=out, x=a, y=b, p_=op: e.tensor_tensor(o, x, y, p_),
            [("sv", 1)], waits)

    def v_stt(self, out, in0, scal, in1, op0, op1, accum=None, waits=()):
        return self.emit(
            "vector",
            lambda e, o=out, x=in0, s=scal, y=in1, p0=op0, p1=op1, ac=accum:
                e.scalar_tensor_tensor(o, x, s, y, p0, p1, accum_out=ac),
            [("sv", 1)], waits)

    def v_ts(self, out, in0, s1, s2, op0, op1, waits=(), fwaits=()):
        if s2 is None:
            return self.emit(
                "vector",
                lambda e, o=out, x=in0, a=s1, p0=op0:
                    e.tensor_single_scalar(o, x, a, p0),
                [("sv", 1)], waits, fwaits)
        return self.emit(
            "vector",
            lambda e, o=out, x=in0, a=s1, b=s2, p0=op0, p1=op1:
                e.tensor_scalar(o, x, a, b, p0, p1),
            [("sv", 1)], waits, fwaits)

    def v_red(self, op, out, in_, waits=()):
        return self.emit(
            "vector",
            lambda e, o=out, x=in_, p_=op: e.tensor_reduce(o, x, AX.X, p_),
            [("sv", 1)], waits)

    def v_copy(self, out, in_, waits=()):
        return self.emit(
            "vector",
            lambda e, o=out, x=in_: e.tensor_copy(o, x),
            [("sv", 1)], waits)

    def v_recip(self, out, in_, waits=()):
        return self.emit(
            "vector",
            lambda e, o=out, x=in_: e.reciprocal(o, x),
            [("sv", 1)], waits)

    def s_act(self, out, in_, func, bias=0.0, scale=1.0, accum=None, waits=()):
        return self.emit(
            "scalar",
            lambda e, o=out, x=in_, f=func, b=bias, s=scale, ac=accum:
                e.activation(o, x, f, bias=b, scale=s, accum_out=ac),
            [("ss", 1)], waits)


def build_nc():
    nc = bass.Bass("TRN2", target_bir_lowering=False)
    p = Plan()

    x_d = nc.dram_tensor("x", [S, D], F32, kind="ExternalInput")
    wq_d = nc.dram_tensor("wq", [D, D], F32, kind="ExternalInput")
    wc_d = nc.dram_tensor("wc", [D, DL], F32, kind="ExternalInput")
    wk_d = nc.dram_tensor("wk", [DL, D], F32, kind="ExternalInput")
    wv_d = nc.dram_tensor("wv", [DL, D], F32, kind="ExternalInput")
    wo_d = nc.dram_tensor("wo", [D, D], F32, kind="ExternalInput")
    wf_d = nc.dram_tensor("wf", [D, D], F32, kind="ExternalInput")
    a2t_d = nc.dram_tensor("a2t", [D, 128], F32, kind="ExternalInput")
    b2_d = nc.dram_tensor("b2", [128, D], F32, kind="ExternalInput")
    rwt_d = nc.dram_tensor("rwt", [D, 8], F32, kind="ExternalInput")
    brep_d = nc.dram_tensor("brep", [128, D], F32, kind="ExternalInput")
    costok_d = nc.dram_tensor("costok", [128, S], F32, kind="ExternalInput")
    sintok_d = nc.dram_tensor("sintok", [128, S], F32, kind="ExternalInput")
    maskc_d = nc.dram_tensor("maskc", [128, NT], F32, kind="ExternalInput")
    identf_d = nc.dram_tensor("identf", [128, 128], F32, kind="ExternalInput")
    onesf_d = nc.dram_tensor("onesf", [128, 1], F32, kind="ExternalInput")
    onesr_d = nc.dram_tensor("onesr", [1, 128], F32, kind="ExternalInput")
    routerb_d = nc.dram_tensor("routerb", [1, 8], F32, kind="ExternalInput")
    consts_d = nc.dram_tensor("consts", [128, 4], F32, kind="ExternalInput")
    out_d = nc.dram_tensor("out", [S, D], F32, kind="ExternalOutput")

    SK = globals().get("SCRATCH_KIND", "Internal")
    ht_s = nc.dram_tensor("ht_s", [NT, 128, D], F32, kind=SK)
    qt_s = nc.dram_tensor("qt_s", [NH, 128, S], F32, kind=SK)
    kt_s = nc.dram_tensor("kt_s", [NH, 128, S], F32, kind=SK)
    v_s = nc.dram_tensor("v_s", [NT, 128, D], F32, kind=SK)
    at_s = nc.dram_tensor("at_s", [NH, 128, S], F32, kind=SK)
    o1_s = nc.dram_tensor("o1_s", [S, D], F32, kind=SK)
    lt_s = nc.dram_tensor("lt_s", [NT, 128, D], F32, kind=SK)
    dbg_d = (nc.dram_tensor("dbg_d", [128, 8192], F32, kind="ExternalOutput")
             if SK != "Internal" else None)

    costok = nc.alloc_sbuf_tensor("costok_sb", [128, S], F32)
    sintok = nc.alloc_sbuf_tensor("sintok_sb", [128, S], F32)
    maskc = nc.alloc_sbuf_tensor("maskc_sb", [128, NT], F32)
    identf = nc.alloc_sbuf_tensor("identf_sb", [128, 128], F32)
    onesf = nc.alloc_sbuf_tensor("onesf_sb", [128, 1], F32)
    onesr = nc.alloc_sbuf_tensor("onesr_sb", [1, 128], F32)
    routerb = nc.alloc_sbuf_tensor("routerb_sb", [1, 8], F32)
    xa = [nc.alloc_sbuf_tensor(f"xa{i}", [128, D], F32) for i in range(2)]
    hbuf = nc.alloc_sbuf_tensor("hbuf", [128, D], F32)
    ht = [nc.alloc_sbuf_tensor(f"ht{i}", [128, D], F32) for i in range(2)]
    slab = nc.alloc_sbuf_tensor("slab", [128, 16384], F32)
    kvt = nc.alloc_sbuf_tensor("kvt", [128, 8192], F32)
    evb = nc.alloc_sbuf_tensor("evb", [128, 2048], F32)
    st1 = nc.alloc_sbuf_tensor("st1", [128, 16], F32)
    st2 = nc.alloc_sbuf_tensor("st2", [128, 8], F32)
    smt = nc.alloc_sbuf_tensor("smt", [1, 64], F32)

    pbS0 = nc.alloc_psum_tensor("pbS0", [128, 1024], F32)
    pbS1 = nc.alloc_psum_tensor("pbS1", [128, 1024], F32)
    pbA = nc.alloc_psum_tensor("pbA", [128, 1024], F32)
    pbZ = nc.alloc_psum_tensor("pbZ", [1, 1024], F32)
    PS = [(pbS0, 0), (pbS0, 512), (pbS1, 0), (pbS1, 512)]
    PA = [(pbA, 0), (pbA, 512)]

    ps_free = {}
    buf_free = {}

    def sb(h, poff, plen, coff, dims):
        fl = h.shape[1]
        return AP(h, poff * fl + coff, [[fl, plen]] + [list(d) for d in dims])

    def psap(slot, coff, dims):
        t, base = slot
        return AP(t, base + coff, [[t.shape[1]] + [dims[0]]] + [list(d) for d in dims[1:]])

    ev_cos = p.din(sb(costok, 0, 128, 0, [[1, S]]), AP(costok_d, 0, [[S, 128], [1, S]]))
    ev_sin = p.din(sb(sintok, 0, 128, 0, [[1, S]]), AP(sintok_d, 0, [[S, 128], [1, S]]))
    ev_msk = p.din(sb(maskc, 0, 128, 0, [[1, NT]]), AP(maskc_d, 0, [[NT, 128], [1, NT]]))
    ev_idf = p.din(sb(identf, 0, 128, 0, [[1, 128]]), AP(identf_d, 0, [[128, 128], [1, 128]]))
    ev_one = p.din(sb(onesf, 0, 128, 0, [[1, 1]]), AP(onesf_d, 0, [[1, 128], [1, 1]]))
    ev_onr = p.din(sb(onesr, 0, 1, 0, [[1, 128]]), AP(onesr_d, 0, [[128, 1], [1, 128]]))
    ev_rb = p.din(sb(routerb, 0, 1, 0, [[1, 8]]), AP(routerb_d, 0, [[8, 1], [1, 8]]))
    consts = nc.alloc_sbuf_tensor("consts_sb", [128, 4], F32)
    ev_cst = p.din(sb(consts, 0, 128, 0, [[1, 4]]), AP(consts_d, 0, [[4, 128], [1, 4]]))

    ident_ap = sb(identf, 0, 128, 0, [[1, 128]])
    ones_ap = sb(onesf, 0, 128, 0, [[1, 1]])
    onesr_ap = sb(onesr, 0, 1, 0, [[1, 128]])
    cEPS = sb(consts, 0, 128, 0, [[1, 1]])
    cNHALF = sb(consts, 0, 128, 1, [[1, 1]])
    cQKS = sb(consts, 0, 128, 2, [[1, 1]])

    def ln_phase(src_dram, dst_scratch, extra_wait):
        dst_ev = [None] * NT
        for i in range(NT):
            par = i % 2
            xap = sb(xa[par], 0, 128, 0, [[1, D]])
            xrd = p.din(xap, AP(src_dram, i * 128 * D, [[D, 128], [1, D]]),
                        [buf_free.get(("xa", par)), extra_wait])
            sums = sb(st1, 0, 128, 0, [[1, 1]])
            ssq = sb(st1, 0, 128, 1, [[1, 1]])
            hfull = sb(hbuf, 0, 128, 0, [[1, D]])
            p.v_red(ALU.add, sums, xap, [xrd])
            e2a = p.s_act(hfull, xap, AF.Square,
                          waits=[xrd, buf_free.get(("hbuf",))])
            e2 = p.v_red(ALU.add, ssq, hfull, [e2a])
            m = sb(st1, 0, 128, 2, [[1, 1]])
            ex2 = sb(st1, 0, 128, 3, [[1, 1]])
            var = sb(st1, 0, 128, 4, [[1, 1]])
            lnv = sb(st1, 0, 128, 5, [[1, 1]])
            rstd = sb(st1, 0, 128, 6, [[1, 1]])
            nmr = sb(st1, 0, 128, 7, [[1, 1]])
            m2 = sb(st1, 0, 128, 8, [[1, 1]])
            m_ev = p.v_ts(m, sums, 1.0 / D, None, ALU.mult, ALU.bypass)
            p.v_ts(ex2, ssq, 1.0 / D, None, ALU.mult, ALU.bypass)
            e_m2 = p.s_act(m2, m, AF.Square, waits=[m_ev])
            e3 = p.v_tt(ALU.subtract, var, ex2, m2, waits=[e_m2])
            e4 = p.s_act(lnv, var, AF.Ln, bias=cEPS, scale=1.0,
                         waits=[e3, ev_cst])
            e5 = p.s_act(rstd, lnv, AF.Exp, scale=cNHALF)
            e6 = p.v_ts(nmr, m, rstd, -1.0, ALU.mult, ALU.mult, waits=[e5])
            e7 = p.v_ts(hfull, xap, rstd, nmr,
                        ALU.mult, ALU.add, waits=[e2], fwaits=[e6])
            buf_free[("xa", par)] = e7
            evacs = []
            for g in range(8):
                si = g % 4
                slot = PS[si]
                trev = None
                for c in range(4):
                    ch = 4 * g + c
                    w = [e7, ev_idf]
                    if c == 0:
                        w.append(ps_free.get(("PS", si)))
                    trev = p.tr(psap(slot, c * 128, [128, [1, 128]]),
                                sb(hbuf, 0, 128, ch * 128, [[1, 128]]),
                                ident_ap, inc=(c == 3), waits=w,
                                start=(c == 0), stop=(c == 3))
                w = [trev]
                if g == 0:
                    w.append(buf_free.get(("ht", par)))
                eev = p.v_copy(sb(ht[par], 0, 128, g * 512, [[1, 512]]),
                               psap(slot, 0, [128, [1, 512]]), waits=w)
                ps_free[("PS", si)] = eev
                evacs.append(eev)
            buf_free[("hbuf",)] = evacs[-1]
            dst_ev[i] = p.dout(AP(dst_scratch, i * 128 * D, [[D, 128], [1, D]]),
                               sb(ht[par], 0, 128, 0, [[1, D]]), waits=evacs)
            buf_free[("ht", par)] = dst_ev[i]
        return dst_ev

    pa_par = [0]

    def rope_evac(slot, slot_id, i, scatter_dst, mm_ev):
        t, base = slot
        fl = t.shape[1]
        ps3 = lambda off, n: AP(t, base + off, [[fl, 128], [128, 4], [1, n]])
        cos3 = lambda off, n: AP(costok, i * 128 + off, [[S, 128], [0, 4], [1, n]])
        sin3 = lambda off, n: AP(sintok, i * 128 + off, [[S, 128], [0, 4], [1, n]])
        tmpc = lambda off, n: AP(evb, off, [[2048, 128], [128, 4], [1, n]])
        tmps = lambda off, n: AP(evb, 512 + off, [[2048, 128], [128, 4], [1, n]])
        eo3 = lambda off, n: AP(evb, 1024 + off, [[2048, 128], [128, 4], [1, n]])
        w0 = [mm_ev, ev_cos, ev_sin, buf_free.get(("ev",))]
        p.v_tt(ALU.mult, tmpc(0, 128), ps3(0, 128), cos3(0, 128), waits=w0)
        p.v_tt(ALU.mult, tmps(0, 64), ps3(64, 64), sin3(0, 64))
        e3 = p.v_tt(ALU.mult, tmps(64, 64), ps3(0, 64), sin3(64, 64))
        ps_free[slot_id] = e3
        p.v_tt(ALU.subtract, eo3(0, 64), tmpc(0, 64), tmps(0, 64))
        e5 = p.v_tt(ALU.add, eo3(64, 64), tmpc(64, 64), tmps(64, 64))
        pid = ("PA", pa_par[0])
        slotA = PA[pa_par[0]]
        pa_par[0] ^= 1
        trev = None
        for c in range(4):
            w = [e5, ev_idf]
            if c == 0:
                w.append(ps_free.get(pid))
            trev = p.tr(psap(slotA, c * 128, [128, [1, 128]]),
                        AP(evb, 1024 + c * 128, [[2048, 128], [1, 128]]),
                        ident_ap, inc=(c == 3), waits=w,
                        start=(c == 0), stop=(c == 3))
        eg = p.v_copy(AP(evb, 1536, [[2048, 128], [1, 512]]),
                      psap(slotA, 0, [128, [1, 512]]),
                      waits=[trev, buf_free.get(("gather",))])
        ps_free[pid] = eg
        buf_free[("ev",)] = eg
        dev = p.dout(scatter_dst, AP(evb, 1536, [[2048, 128], [128, 4], [1, 128]]),
                     waits=[eg])
        buf_free[("gather",)] = dev
        return dev

    ht_ev = ln_phase(x_d, ht_s, None)

    qt_write_evs = []
    kvT_ready = []
    slab_free = None
    last_mm = None
    for sj in range(9):
        if sj == 0:
            srcs = [AP(wc_d, k * 8 * 128 * DL, [[DL, 128], [128 * DL, 8], [1, DL]])
                    for k in range(4)]
        else:
            srcs = [AP(wq_d, (sj - 1) * OS + k * 8 * 128 * D,
                       [[D, 128], [128 * D, 8], [1, OS]]) for k in range(4)]
        sl_evs = [p.din(sb(slab, 0, 128, k * 8 * OS, [[OS, 8], [1, OS]]),
                        srcs[k], [slab_free]) for k in range(4)]
        for i in range(NT):
            par = i % 2
            htev = p.din(sb(ht[par], 0, 128, 0, [[1, D]]),
                         AP(ht_s, i * 128 * D, [[D, 128], [1, D]]),
                         [buf_free.get(("ht", par)), ht_ev[i]])
            si = (sj * NT + i) % 4
            slot = PS[si]
            mm_ev = None
            for c in range(ND):
                w = ([htev] + sl_evs + [ps_free.get(("PS", si))]) if c == 0 else []
                mm_ev = p.mm(psap(slot, 0, [128, [1, OS]]),
                             sb(ht[par], 0, 128, c * 128, [[1, 128]]),
                             sb(slab, 0, 128, c * OS, [[1, OS]]),
                             start=(c == 0), stop=(c == ND - 1),
                             inc=(c == ND - 1), waits=w)
            last_mm = mm_ev
            buf_free[("ht", par)] = mm_ev
            if sj == 0:
                eg0 = p.v_copy(AP(evb, 1536, [[2048, 128], [1, 512]]),
                               psap(slot, 0, [128, [1, 512]]),
                               waits=[mm_ev, buf_free.get(("gather",))])
                ps_free[("PS", si)] = eg0
                pid = ("PA", pa_par[0])
                slotA = PA[pa_par[0]]
                pa_par[0] ^= 1
                trev = None
                for c in range(4):
                    w = [eg0, ev_idf]
                    if c == 0:
                        w.append(ps_free.get(pid))
                    trev = p.tr(psap(slotA, c * 128, [128, [1, 128]]),
                                AP(evb, 1536 + c * 128, [[2048, 128], [1, 128]]),
                                ident_ap, inc=(c == 3), waits=w,
                                start=(c == 0), stop=(c == 3))
                buf_free[("gather",)] = trev
                ekv = p.v_copy(AP(kvt, i * 128, [[8192, 128], [2048, 4], [1, 128]]),
                               AP(slotA[0], slotA[1],
                                  [[slotA[0].shape[1], 128], [128, 4], [1, 128]]),
                               waits=[trev])
                ps_free[pid] = ekv
                kvT_ready.append(ekv)
            else:
                hb = 4 * (sj - 1)
                scatter = AP(qt_s, hb * 128 * S + i * 128,
                             [[S, 128], [128 * S, 4], [1, 128]])
                qt_write_evs.append(rope_evac(slot, ("PS", si), i, scatter, mm_ev))
        slab_free = last_mm

    kt_write_evs = []
    v_write_evs = []
    for sj in range(16):
        isk = sj < 8
        j = sj if isk else sj - 8
        wsrc_d = wk_d if isk else wv_d
        sl_ev = p.din(sb(slab, 0, 128, 0, [[OS, 4], [1, OS]]),
                      AP(wsrc_d, j * OS, [[D, 128], [128 * D, 4], [1, OS]]),
                      [slab_free])
        for i in range(NT):
            si = i % 4
            slot = PS[si]
            mm_ev = None
            for c in range(NL):
                w = ([sl_ev] + kvT_ready + [ps_free.get(("PS", si))]) if c == 0 else []
                mm_ev = p.mm(psap(slot, 0, [128, [1, OS]]),
                             AP(kvt, c * 2048 + i * 128, [[8192, 128], [1, 128]]),
                             sb(slab, 0, 128, c * OS, [[1, OS]]),
                             start=(c == 0), stop=(c == NL - 1),
                             inc=(c == NL - 1), waits=w)
            last_mm = mm_ev
            if isk:
                hb = 4 * j
                scatter = AP(kt_s, hb * 128 * S + i * 128,
                             [[S, 128], [128 * S, 4], [1, 128]])
                kt_write_evs.append(rope_evac(slot, ("PS", si), i, scatter, mm_ev))
            else:
                eg0 = p.v_copy(AP(evb, 1536, [[2048, 128], [1, 512]]),
                               psap(slot, 0, [128, [1, 512]]),
                               waits=[mm_ev, buf_free.get(("gather",))])
                ps_free[("PS", si)] = eg0
                dev = p.dout(AP(v_s, i * 128 * D + j * OS, [[D, 128], [1, OS]]),
                             AP(evb, 1536, [[2048, 128], [1, 512]]), waits=[eg0])
                buf_free[("gather",)] = dev
                v_write_evs.append(dev)
        slab_free = last_mm

    phaseA_end = [qt_write_evs[-1], kt_write_evs[-1], v_write_evs[-1]]
    a3_t = p.now("tensor", "st")
    buf_free[("bq", 0)] = buf_free[("bq", 1)] = a3_t
    ps_free[("psT", 0)] = ps_free[("psT", 1)] = a3_t
    buf_free[("zrep",)] = buf_free[("atev",)] = a3_t

    at_write_evs = []
    qkv_ev = {}
    for h in range(NH + 1):
        if h < NH:
            ph = h % 2
            base = ph * 6144
            w = [buf_free.get(("bq", ph))] + (phaseA_end if h < 2 else [])
            e_q = p.din(AP(slab, base, [[16384, 128], [1, S]]),
                        AP(qt_s, h * 128 * S, [[S, 128], [1, S]]), w)
            e_k = p.din(AP(slab, base + 2048, [[16384, 128], [1, S]]),
                        AP(kt_s, h * 128 * S, [[S, 128], [1, S]]), w)
            e_v = [p.din(AP(slab, base + 4096 + k * 1024,
                            [[16384, 128], [128, 8], [1, 128]]),
                         AP(v_s, h * 128 + k * 8 * 128 * D,
                            [[D, 128], [128 * D, 8], [1, 128]]), w)
                   for k in range(2)]
            qkv_ev[h] = [e_q, e_k] + e_v
        if h == 0:
            continue
        hh = h - 1
        ph = hh % 2
        base = ph * 6144
        for g in range(2):
            last_z = None
            for j in range(16):
                pj = j % 2
                pbS = pbS0 if pj == 0 else pbS1
                psT_off = pj * 1024
                wqk = qkv_ev[hh] + [ps_free.get(("BS", pj))]
                for half in range(2):
                    p.mm(AP(pbS, half * 512, [[1024, 128], [1, 512]]),
                         AP(slab, base + 2048 + j * 128, [[16384, 128], [1, 128]]),
                         AP(slab, base + g * 1024 + half * 512,
                            [[16384, 128], [1, 512]]),
                         start=True, stop=True, inc=(half == 1),
                         waits=wqk if half == 0 else [])
                e_qk = p.now("tensor", "st")
                e_ex = p.s_act(AP(kvt, psT_off, [[8192, 128], [1, 1024]]),
                               AP(pbS, 0, [[1024, 128], [1, 1024]]),
                               AF.Exp, scale=cQKS,
                               waits=[e_qk, ps_free.get(("psT", pj)), ev_cst])
                ps_free[("BS", pj)] = e_ex
                for half in range(2):
                    w = [e_ex]
                    if j == 0 and half == 0:
                        w += [ps_free.get(("pbA",)), ps_free.get(("pbZ",))]
                    p.mm(AP(pbA, half * 512, [[1024, 128], [1, 512]]),
                         AP(slab, base + 4096 + j * 128, [[16384, 128], [1, 128]]),
                         AP(kvt, psT_off + half * 512, [[8192, 128], [1, 512]]),
                         start=(j == 0), stop=(j == 15), inc=False, waits=w)
                    last_z = p.mm(AP(pbZ, half * 512, [[1024, 1], [1, 512]]),
                                  ones_ap,
                                  AP(kvt, psT_off + half * 512, [[8192, 128], [1, 512]]),
                                  start=(j == 0), stop=(j == 15), inc=(half == 1),
                                  waits=[ev_one] if (j == 0 and half == 0) else [])
                ps_free[("psT", pj)] = last_z
            e_zc = p.v_copy(AP(kvt, 4096, [[8192, 1], [1, 1024]]),
                            AP(pbZ, 0, [[1024, 1], [1, 1024]]), waits=[last_z])
            ps_free[("pbZ",)] = e_zc
            e_zr = p.v_recip(AP(kvt, 5120, [[8192, 1], [1, 1024]]),
                             AP(kvt, 4096, [[8192, 1], [1, 1024]]))
            bc_ev = None
            for half in range(2):
                w = ([e_zr, ev_onr, ps_free.get(("BS", 0))]
                     if half == 0 else [])
                bc_ev = p.mm(AP(pbS0, half * 512, [[1024, 128], [1, 512]]),
                             onesr_ap,
                             AP(kvt, 5120 + half * 512, [[8192, 1], [1, 512]]),
                             start=True, stop=True, inc=(half == 1), waits=w)
            e_zv = p.v_copy(AP(kvt, 2048, [[8192, 128], [1, 1024]]),
                            AP(pbS0, 0, [[1024, 128], [1, 1024]]),
                            waits=[bc_ev, buf_free.get(("zrep",))])
            ps_free[("BS", 0)] = e_zv
            e_at = p.v_tt(ALU.mult, AP(kvt, 3072, [[8192, 128], [1, 1024]]),
                          AP(pbA, 0, [[1024, 128], [1, 1024]]),
                          AP(kvt, 2048, [[8192, 128], [1, 1024]]),
                          waits=[e_zv, buf_free.get(("atev",)), last_z])
            ps_free[("pbA",)] = e_at
            buf_free[("zrep",)] = e_at
            e_ad = p.dout(AP(at_s, hh * 128 * S + g * 1024, [[S, 128], [1, 1024]]),
                          AP(kvt, 3072, [[8192, 128], [1, 1024]]), waits=[e_at])
            buf_free[("atev",)] = e_ad
            at_write_evs.append(e_ad)
        buf_free[("bq", ph)] = p.now("tensor", "st")

    b_end_v = p.now("vector", "sv")
    b_end_out = p.now("gpsimd", "sout")
    buf_free[("xsl", 0)] = buf_free[("xsl", 1)] = b_end_v
    buf_free[("eo",)] = b_end_v
    buf_free[("ev",)] = b_end_v
    buf_free[("gather",)] = b_end_v

    o1_tile_ev = {}
    slab_free = p.now("tensor", "st")
    for sj in range(8):
        sl_evs = [p.din(sb(slab, 0, 128, k * 8 * OS, [[OS, 8], [1, OS]]),
                        AP(wo_d, sj * OS + k * 8 * 128 * D,
                           [[D, 128], [128 * D, 8], [1, OS]]),
                        [slab_free]) for k in range(4)]
        for i in range(NT):
            par = i % 2
            w = [buf_free.get(("ht", par))] + (at_write_evs if (sj == 0 and i == 0) else [])
            a_evs = [p.din(sb(ht[par], 0, 128, k * 1024, [[128, 8], [1, 128]]),
                           AP(at_s, (8 * k) * 128 * S + i * 128,
                              [[S, 128], [128 * S, 8], [1, 128]]), w)
                     for k in range(4)]
            exs = p.din(AP(evb, par * 512, [[2048, 128], [1, 512]]),
                        AP(x_d, i * 128 * D + sj * OS, [[D, 128], [1, OS]]),
                        [buf_free.get(("xsl", par))])
            si = i % 4
            slot = PS[si]
            mm_ev = None
            for c in range(ND):
                w = (a_evs + sl_evs + [ps_free.get(("PS", si))]) if c == 0 else []
                mm_ev = p.mm(psap(slot, 0, [128, [1, OS]]),
                             sb(ht[par], 0, 128, c * 128, [[1, 128]]),
                             sb(slab, 0, 128, c * OS, [[1, OS]]),
                             start=(c == 0), stop=(c == ND - 1),
                             inc=(c == ND - 1), waits=w)
            last_mm = mm_ev
            buf_free[("ht", par)] = mm_ev
            e_o1 = p.v_tt(ALU.add, AP(evb, 1024, [[2048, 128], [1, 512]]),
                          psap(slot, 0, [128, [1, 512]]),
                          AP(evb, par * 512, [[2048, 128], [1, 512]]),
                          waits=[mm_ev, exs, buf_free.get(("eo",))])
            ps_free[("PS", si)] = e_o1
            buf_free[("xsl", par)] = e_o1
            dev = p.dout(AP(o1_s, i * 128 * D + sj * OS, [[D, 128], [1, OS]]),
                         AP(evb, 1024, [[2048, 128], [1, 512]]), waits=[e_o1])
            buf_free[("eo",)] = dev
            o1_tile_ev[(sj, i)] = dev
        slab_free = last_mm

    c1_last = o1_tile_ev[(7, NT - 1)]

    lt_ev = ln_phase(o1_s, lt_s, c1_last)

    e_a2 = [p.din(sb(slab, 0, 128, k * 8 * 128, [[128, 8], [1, 128]]),
                  AP(a2t_d, k * 8 * 128 * 128, [[128, 128], [128 * 128, 8], [1, 128]]),
                  [slab_free]) for k in range(4)]
    e_rw = p.din(sb(slab, 0, 128, 4096, [[8, 32], [1, 8]]),
                 AP(rwt_d, 0, [[8, 128], [128 * 8, 32], [1, 8]]), [slab_free])
    t_evs = []
    rl_evs = []
    mm_r = None
    for i in range(NT):
        par = i % 2
        htev = p.din(sb(ht[par], 0, 128, 0, [[1, D]]),
                     AP(lt_s, i * 128 * D, [[D, 128], [1, D]]),
                     [buf_free.get(("ht", par)), lt_ev[i]])
        mm_t = None
        for c in range(ND):
            w = ([htev] + e_a2 + [e_rw, ps_free.get(("PS", 0))]) if c == 0 else []
            w2 = [ps_free.get(("PS", 2))] if c == 0 else []
            mm_t = p.mm(AP(pbS0, 0, [[1024, 128], [1, 128]]),
                        sb(ht[par], 0, 128, c * 128, [[1, 128]]),
                        sb(slab, 0, 128, c * 128, [[1, 128]]),
                        start=(c == 0), stop=(c == ND - 1),
                        inc=(c == ND - 1), waits=w)
            mm_r = p.mm(AP(pbS1, 0, [[1024, 128], [1, 8]]),
                        sb(ht[par], 0, 128, c * 128, [[1, 128]]),
                        sb(slab, 0, 128, 4096 + c * 8, [[1, 8]]),
                        start=(c == 0), stop=(c == ND - 1),
                        inc=(c == ND - 1), waits=w2)
        buf_free[("ht", par)] = mm_r
        e_t = p.v_copy(AP(kvt, i * 128, [[8192, 128], [1, 128]]),
                       AP(pbS0, 0, [[1024, 128], [1, 128]]), waits=[mm_t])
        ps_free[("PS", 0)] = e_t
        e_rl = p.v_copy(AP(evb, i * 8, [[2048, 128], [1, 8]]),
                        AP(pbS1, 0, [[1024, 128], [1, 8]]), waits=[mm_r])
        ps_free[("PS", 2)] = e_rl
        t_evs.append(e_t)
        rl_evs.append(e_rl)
    lg_mm = None
    for i in range(NT):
        w = [rl_evs[i], ev_msk] + ([ps_free.get(("pbZ",))] if i == 0 else [])
        lg_mm = p.mm(AP(pbZ, 0, [[1024, 1], [1, 8]]),
                     sb(maskc, 0, 128, i, [[1, 1]]),
                     AP(evb, i * 8, [[2048, 128], [1, 8]]),
                     start=(i == 0), stop=(i == NT - 1),
                     inc=(i == NT - 1), waits=w)
    lg = AP(smt, 0, [[64, 1], [1, 8]])
    mx = AP(smt, 8, [[64, 1], [1, 1]])
    nmx = AP(smt, 9, [[64, 1], [1, 1]])
    pr = AP(smt, 16, [[64, 1], [1, 8]])
    sm1 = AP(smt, 24, [[64, 1], [1, 1]])
    rsm = AP(smt, 25, [[64, 1], [1, 1]])
    rout = AP(smt, 32, [[64, 1], [1, 8]])
    e1 = p.v_tt(ALU.add, lg, AP(pbZ, 0, [[1024, 1], [1, 8]]),
                AP(routerb, 0, [[8, 1], [1, 8]]), waits=[lg_mm, ev_rb])
    ps_free[("pbZ",)] = e1
    p.v_red(ALU.max, mx, lg)
    p.v_ts(nmx, mx, -1.0, None, ALU.mult, ALU.bypass)
    e2 = p.s_act(pr, lg, AF.Exp, bias=nmx, scale=1.0, accum=sm1,
                 waits=[p.now("vector", "sv")])
    e3r = p.v_recip(rsm, sm1, waits=[e2])
    e4 = p.v_ts(rout, pr, rsm, 2.0, ALU.mult, ALU.mult, waits=[e2],
                fwaits=[e3r])
    e5 = p.mm(AP(pbS0, 0, [[1024, 128], [1, 8]]), onesr_ap, rout,
              start=True, stop=True, inc=True,
              waits=[e4, ev_onr, ps_free.get(("PS", 0))])
    e6 = p.v_copy(AP(st2, 0, [[8, 128], [1, 8]]),
                  AP(pbS0, 0, [[1024, 128], [1, 8]]), waits=[e5])
    ps_free[("PS", 0)] = e6
    e7 = None
    for ee in range(8):
        e7 = p.v_ts(AP(evb, ee * 16, [[2048, 128], [128, 16], [1, 16]]),
                    AP(kvt, ee * 16, [[8192, 128], [128, 16], [1, 16]]),
                    sb(st2, 0, 128, ee, [[1, 1]]), None, ALU.mult, ALU.bypass,
                    waits=t_evs if ee == 0 else [],
                    fwaits=[e6] if ee == 0 else [])
    ts_evs = []
    for gg in range(4):
        pid = ("PA", gg % 2)
        slotA = PA[gg % 2]
        trev = None
        for c in range(4):
            i = 4 * gg + c
            w = [e7, ev_idf] + ([ps_free.get(pid)] if c == 0 else [])
            trev = p.tr(psap(slotA, c * 128, [128, [1, 128]]),
                        AP(evb, i * 128, [[2048, 128], [1, 128]]),
                        ident_ap, inc=(c == 3), waits=w,
                        start=(c == 0), stop=(c == 3))
        e8 = p.v_copy(AP(kvt, 2048 + gg * 512, [[8192, 128], [1, 512]]),
                      psap(slotA, 0, [128, [1, 512]]), waits=[trev])
        ps_free[pid] = e8
        ts_evs.append(e8)
    if dbg_d is not None:
        p.dout(AP(dbg_d, 0, [[8192, 128], [1, 2048]]),
               AP(evb, 0, [[2048, 128], [1, 2048]]), waits=[ts_evs[-1]])
        p.dout(AP(dbg_d, 2048, [[8192, 128], [1, 2048]]),
               AP(kvt, 2048, [[8192, 128], [1, 2048]]), waits=[ts_evs[-1]])
        p.dout(AP(dbg_d, 4096, [[8192, 128], [1, 2048]]),
               AP(kvt, 0, [[8192, 128], [1, 2048]]), waits=[ts_evs[-1]])
        p.dout(AP(dbg_d, 6144, [[8192, 128], [1, 8]]),
               AP(st2, 0, [[8, 128], [1, 8]]), waits=[ts_evs[-1], e6])
    slab_free = mm_r
    c2b_t = p.now("tensor", "st")
    buf_free[("brep",)] = buf_free[("tmp",)] = buf_free[("eo2",)] = c2b_t
    buf_free[("xo1", 0)] = buf_free[("xo1", 1)] = c2b_t

    e_b2 = p.din(AP(kvt, 4096, [[8192, 128], [1, D]]),
                 AP(b2_d, 0, [[D, 128], [1, D]]), ts_evs + [b_end_out])
    out_evs = []
    e_t1 = None
    for sj in range(8):
        sl_evs = [p.din(sb(slab, 0, 128, k * 8 * OS, [[OS, 8], [1, OS]]),
                        AP(wf_d, sj * OS + k * 8 * 128 * D,
                           [[D, 128], [128 * D, 8], [1, OS]]),
                        [slab_free]) for k in range(4)]
        e_br = p.din(AP(hbuf, 1024, [[D, 128], [1, 512]]),
                     AP(brep_d, sj * OS, [[D, 128], [1, OS]]),
                     [buf_free.get(("brep",))])
        for i in range(NT):
            par = i % 2
            htev = p.din(sb(ht[par], 0, 128, 0, [[1, D]]),
                         AP(lt_s, i * 128 * D, [[D, 128], [1, D]]),
                         [buf_free.get(("ht", par)), lt_ev[i]])
            exs = p.din(AP(hbuf, par * 512, [[D, 128], [1, 512]]),
                        AP(o1_s, i * 128 * D + sj * OS, [[D, 128], [1, OS]]),
                        [buf_free.get(("xo1", par)), o1_tile_ev[(sj, i)]])
            si = i % 4
            slot = PS[si]
            mm_ev = None
            for c in range(ND):
                w = ([htev] + sl_evs + [ps_free.get(("PS", si))]) if c == 0 else []
                mm_ev = p.mm(psap(slot, 0, [128, [1, OS]]),
                             sb(ht[par], 0, 128, c * 128, [[1, 128]]),
                             sb(slab, 0, 128, c * OS, [[1, OS]]),
                             start=(c == 0), stop=False, inc=False, waits=w)
            mm_ev = p.mm(psap(slot, 0, [128, [1, OS]]),
                         AP(kvt, 2048 + i * 128, [[8192, 128], [1, 128]]),
                         AP(kvt, 4096 + sj * OS, [[8192, 128], [1, OS]]),
                         start=False, stop=True, inc=True, waits=[e_b2])
            last_mm = mm_ev
            buf_free[("ht", par)] = mm_ev
            e_t1 = p.v_tt(ALU.add, AP(hbuf, 1536, [[D, 128], [1, 512]]),
                          psap(slot, 0, [128, [1, 512]]),
                          AP(hbuf, 1024, [[D, 128], [1, 512]]),
                          waits=[mm_ev, e_br, buf_free.get(("tmp",))])
            ps_free[("PS", si)] = e_t1
            e_t2 = p.v_tt(ALU.add, AP(hbuf, 2048, [[D, 128], [1, 512]]),
                          AP(hbuf, 1536, [[D, 128], [1, 512]]),
                          AP(hbuf, par * 512, [[D, 128], [1, 512]]),
                          waits=[exs, buf_free.get(("eo2",))])
            buf_free[("tmp",)] = e_t2
            buf_free[("xo1", par)] = e_t2
            dev = p.dout(AP(out_d, i * 128 * D + sj * OS, [[D, 128], [1, OS]]),
                         AP(hbuf, 2048, [[D, 128], [1, 512]]), waits=[e_t2])
            buf_free[("eo2",)] = dev
            out_evs.append(dev)
        slab_free = last_mm
        buf_free[("brep",)] = e_t1

    p.ops["gpsimd"].append(("w", "sout", p.cnt["sout"]))

    with (
        nc.semaphore("sin") as h_sin,
        nc.semaphore("sout") as h_sout,
        nc.semaphore("st") as h_st,
        nc.semaphore("sv") as h_sv,
        nc.semaphore("ss") as h_ss,
        nc.Block() as block,
    ):
        sems = {"sin": h_sin, "sout": h_sout, "st": h_st, "sv": h_sv, "ss": h_ss}

        def mkbody(eng_ops):
            def body(eng):
                for op in eng_ops:
                    if op[0] == "w":
                        eng.wait_ge(sems[op[1]], op[2])
                    else:
                        inst = op[1](eng)
                        for (s, k) in op[2]:
                            inst.then_inc(sems[s], k)
            return body

        block.tensor(mkbody(p.ops["tensor"]))
        block.vector(mkbody(p.ops["vector"]))
        block.scalar(mkbody(p.ops["scalar"]))
        block.sync(mkbody(p.ops["sync"]))
        block.gpsimd(mkbody(p.ops["gpsimd"]))

    return nc


def _host_inputs(inputs):
    f32 = np.float32
    Wq = np.ascontiguousarray(inputs["L0_attn_Wq"], f32)
    Wc = np.ascontiguousarray(inputs["L0_attn_Wc"], f32)
    Wk = np.ascontiguousarray(inputs["L0_attn_Wk"], f32)
    Wv = np.ascontiguousarray(inputs["L0_attn_Wv"], f32)
    Wo = np.ascontiguousarray(inputs["L0_attn_Wo"], f32)
    Wf = np.ascontiguousarray(np.asarray(inputs["L0_ffn_W"]).T, f32)
    A = np.asarray(inputs["L0_ffn_A"], f32)
    B = np.asarray(inputs["L0_ffn_B"], f32)
    A2T = np.ascontiguousarray(A.reshape(128, D).T, f32)
    B2 = np.ascontiguousarray(np.transpose(B, (0, 2, 1)).reshape(128, D), f32)
    RWT = np.ascontiguousarray(np.asarray(inputs["L0_ffn_router_W"]).T, f32)
    brep = np.ascontiguousarray(
        np.broadcast_to(np.asarray(inputs["L0_ffn_b"], f32), (128, D)), f32)
    routerb = np.ascontiguousarray(
        np.asarray(inputs["L0_ffn_router_b"], f32).reshape(1, 8), f32)

    pos = np.arange(S, dtype=np.float64)
    inv = 1.0 / (10000.0 ** (np.arange(0, 128, 2, dtype=np.float64) / 128.0))
    ang = pos[:, None] * inv[None, :]
    ang = np.concatenate([ang, ang], axis=1)
    cos = np.cos(ang).astype(f32)
    sin = np.sin(ang).astype(f32)
    costok = np.zeros((128, S), f32)
    sintok = np.zeros((128, S), f32)
    for i in range(NT):
        costok[:, i * 128:(i + 1) * 128] = cos[i * 128:(i + 1) * 128, :]
        sintok[:, i * 128:(i + 1) * 128] = sin[i * 128:(i + 1) * 128, :]

    maskc = np.zeros((128, NT), f32)
    for t in range(611, 2047):
        maskc[t % 128, t // 128] = np.float32(np.float32(1.0) / np.float32(1436.0))

    identf = np.eye(128, dtype=f32)
    onesf = np.ones((128, 1), f32)
    onesr = np.ones((1, 128), f32)
    consts = np.zeros((128, 4), f32)
    consts[:, 0] = NEPS
    consts[:, 1] = -0.5
    consts[:, 2] = QK_SCALE

    return dict(wq=Wq, wc=Wc, wk=Wk, wv=Wv, wo=Wo, wf=Wf, a2t=A2T, b2=B2,
                rwt=RWT, brep=brep, routerb=routerb, costok=costok,
                sintok=sintok, maskc=maskc, identf=identf, onesf=onesf,
                onesr=onesr, consts=consts)


def kernel(**inputs):
    xfull = np.ascontiguousarray(inputs["L0_attn_hidden_states"], np.float32)
    com = _host_inputs(inputs)
    nc = build_nc()
    in_maps = [dict(com, x=xfull[b]) for b in range(8)]
    res = run_bass_kernel_spmd(nc, in_maps, core_ids=list(range(8)))
    return np.stack([res.results[b]["out"] for b in range(8)], axis=0)


# version 04
# speedup vs baseline: 1.1443x; 1.1443x over previous
import numpy as np
import concourse.bass as bass
import concourse.mybir as mybir
from concourse.bass_utils import run_bass_kernel_spmd

F32 = mybir.dt.float32
BF16 = mybir.dt.bfloat16
AF = mybir.ActivationFunctionType
ALU = mybir.AluOpType
AX = mybir.AxisListType

S, D, DL = 2048, 4096, 512
NT, ND, NL, NH = 16, 32, 4, 32
OS = 512
NEPS = 1e-5
QK_SCALE = float(1.0 / np.sqrt(128.0))

ENGINES = ("tensor", "vector", "scalar", "sync", "gpsimd")
DMA_SEMS = ("sin", "sout")


def AP(h, off, dims):
    return bass.AP(h, off, [list(d) for d in dims])


class Plan:
    def __init__(self):
        self.ops = {e: [] for e in ENGINES}
        self.cnt = {"sin": 0, "sout": 0, "st": 0, "sv": 0, "ss": 0}

    def now(self, eng, sem):
        return (eng, sem, self.cnt[sem])

    def emit(self, eng, fn, incs=(), waits=(), fwaits=()):
        lst = self.ops[eng]
        merged = {}
        for w in waits:
            if w is None:
                continue
            weng, wsem, wval = w
            if weng == eng and wsem not in DMA_SEMS:
                continue
            if wval <= 0:
                continue
            merged[wsem] = max(merged.get(wsem, 0), wval)
        for w in fwaits:
            if w is None:
                continue
            weng, wsem, wval = w
            if wval <= 0:
                continue
            merged[wsem] = max(merged.get(wsem, 0), wval)
        for wsem, wval in merged.items():
            lst.append(("w", wsem, wval))
        lst.append(("i", fn, tuple(incs)))
        ev = None
        for (s, k) in incs:
            self.cnt[s] += k
            ev = (eng, s, self.cnt[s])
        return ev

    def din(self, dst, src, waits=(), fence=True):
        ev = self.emit("sync",
                       lambda e, d=dst, s=src: e.dma_start(out=d, in_=s),
                       [("sin", 16)], waits)
        if fence:
            self.ops["sync"].append(("w", "sin", ev[2]))
        return ev

    def dout(self, dst, src, waits=()):
        return self.emit("gpsimd",
                         lambda e, d=dst, s=src: e.dma_start(out=d, in_=s),
                         [("sout", 16)], waits)

    def mm(self, out, lhsT, rhs, start, stop, inc=False, waits=()):
        return self.emit(
            "tensor",
            lambda e, o=out, l=lhsT, r=rhs, a=start, b=stop:
                e.matmul(o, l, r, start=a, stop=b),
            [("st", 1)] if inc else [], waits)

    def tr(self, out, in_, ident, inc=False, waits=(), start=True, stop=True):
        return self.emit(
            "tensor",
            lambda e, o=out, i=in_, d=ident, a=start, b=stop:
                e.matmul(o, i, d, start=a, stop=b, is_transpose=True),
            [("st", 1)] if inc else [], waits)

    def v_tt(self, op, out, a, b, waits=()):
        return self.emit(
            "vector",
            lambda e, o=out, x=a, y=b, p_=op: e.tensor_tensor(o, x, y, p_),
            [("sv", 1)], waits)

    def v_stt(self, out, in0, scal, in1, op0, op1, accum=None, waits=()):
        return self.emit(
            "vector",
            lambda e, o=out, x=in0, s=scal, y=in1, p0=op0, p1=op1, ac=accum:
                e.scalar_tensor_tensor(o, x, s, y, p0, p1, accum_out=ac),
            [("sv", 1)], waits)

    def v_ts(self, out, in0, s1, s2, op0, op1, waits=(), fwaits=()):
        if s2 is None:
            return self.emit(
                "vector",
                lambda e, o=out, x=in0, a=s1, p0=op0:
                    e.tensor_single_scalar(o, x, a, p0),
                [("sv", 1)], waits, fwaits)
        return self.emit(
            "vector",
            lambda e, o=out, x=in0, a=s1, b=s2, p0=op0, p1=op1:
                e.tensor_scalar(o, x, a, b, p0, p1),
            [("sv", 1)], waits, fwaits)

    def v_red(self, op, out, in_, waits=()):
        return self.emit(
            "vector",
            lambda e, o=out, x=in_, p_=op: e.tensor_reduce(o, x, AX.X, p_),
            [("sv", 1)], waits)

    def v_copy(self, out, in_, waits=()):
        return self.emit(
            "vector",
            lambda e, o=out, x=in_: e.tensor_copy(o, x),
            [("sv", 1)], waits)

    def v_recip(self, out, in_, waits=()):
        return self.emit(
            "vector",
            lambda e, o=out, x=in_: e.reciprocal(o, x),
            [("sv", 1)], waits)

    def s_act(self, out, in_, func, bias=0.0, scale=1.0, accum=None, waits=()):
        return self.emit(
            "scalar",
            lambda e, o=out, x=in_, f=func, b=bias, s=scale, ac=accum:
                e.activation(o, x, f, bias=b, scale=s, accum_out=ac),
            [("ss", 1)], waits)


def build_nc():
    nc = bass.Bass("TRN2", target_bir_lowering=False)
    p = Plan()

    x_d = nc.dram_tensor("x", [S, D], F32, kind="ExternalInput")
    wq_d = nc.dram_tensor("wq", [D, D], F32, kind="ExternalInput")
    wc_d = nc.dram_tensor("wc", [D, DL], F32, kind="ExternalInput")
    wk_d = nc.dram_tensor("wk", [DL, D], F32, kind="ExternalInput")
    wv_d = nc.dram_tensor("wv", [DL, D], F32, kind="ExternalInput")
    wo_d = nc.dram_tensor("wo", [D, D], F32, kind="ExternalInput")
    wf_d = nc.dram_tensor("wf", [D, D], F32, kind="ExternalInput")
    a2t_d = nc.dram_tensor("a2t", [D, 128], F32, kind="ExternalInput")
    b2_d = nc.dram_tensor("b2", [128, D], F32, kind="ExternalInput")
    rwt_d = nc.dram_tensor("rwt", [D, 8], F32, kind="ExternalInput")
    brep_d = nc.dram_tensor("brep", [128, D], F32, kind="ExternalInput")
    costok_d = nc.dram_tensor("costok", [128, S], F32, kind="ExternalInput")
    sintok_d = nc.dram_tensor("sintok", [128, S], F32, kind="ExternalInput")
    maskc_d = nc.dram_tensor("maskc", [128, NT], F32, kind="ExternalInput")
    identf_d = nc.dram_tensor("identf", [128, 128], F32, kind="ExternalInput")
    onesf_d = nc.dram_tensor("onesf", [128, 1], F32, kind="ExternalInput")
    onesr_d = nc.dram_tensor("onesr", [1, 128], F32, kind="ExternalInput")
    routerb_d = nc.dram_tensor("routerb", [1, 8], F32, kind="ExternalInput")
    consts_d = nc.dram_tensor("consts", [128, 4], F32, kind="ExternalInput")
    out_d = nc.dram_tensor("out", [S, D], F32, kind="ExternalOutput")

    SK = globals().get("SCRATCH_KIND", "Internal")
    ht_s = nc.dram_tensor("ht_s", [NT, 128, D], F32, kind=SK)
    qt_s = nc.dram_tensor("qt_s", [NH, 128, S], F32, kind=SK)
    kt_s = nc.dram_tensor("kt_s", [NH, 128, S], F32, kind=SK)
    v_s = nc.dram_tensor("v_s", [NT, 128, D], F32, kind=SK)
    at_s = nc.dram_tensor("at_s", [NH, 128, S], F32, kind=SK)
    o1_s = nc.dram_tensor("o1_s", [S, D], F32, kind=SK)
    lt_s = nc.dram_tensor("lt_s", [NT, 128, D], F32, kind=SK)
    dbg_d = (nc.dram_tensor("dbg_d", [128, 8192], F32, kind="ExternalOutput")
             if SK != "Internal" else None)

    costok = nc.alloc_sbuf_tensor("costok_sb", [128, S], F32)
    sintok = nc.alloc_sbuf_tensor("sintok_sb", [128, S], F32)
    maskc = nc.alloc_sbuf_tensor("maskc_sb", [128, NT], F32)
    identf = nc.alloc_sbuf_tensor("identf_sb", [128, 128], F32)
    onesf = nc.alloc_sbuf_tensor("onesf_sb", [128, 1], F32)
    onesr = nc.alloc_sbuf_tensor("onesr_sb", [1, 128], F32)
    routerb = nc.alloc_sbuf_tensor("routerb_sb", [1, 8], F32)
    xa = [nc.alloc_sbuf_tensor(f"xa{i}", [128, D], F32) for i in range(2)]
    hbuf = nc.alloc_sbuf_tensor("hbuf", [128, D], F32)
    ht = [nc.alloc_sbuf_tensor(f"ht{i}", [128, D], F32) for i in range(2)]
    slab = nc.alloc_sbuf_tensor("slab", [128, 16384], F32)
    kvt = nc.alloc_sbuf_tensor("kvt", [128, 8192], F32)
    evb = nc.alloc_sbuf_tensor("evb", [128, 2048], F32)
    st1 = nc.alloc_sbuf_tensor("st1", [128, 16], F32)
    st2 = nc.alloc_sbuf_tensor("st2", [128, 8], F32)
    smt = nc.alloc_sbuf_tensor("smt", [1, 64], F32)

    pbS0 = nc.alloc_psum_tensor("pbS0", [128, 1024], F32)
    pbS1 = nc.alloc_psum_tensor("pbS1", [128, 1024], F32)
    pbA = nc.alloc_psum_tensor("pbA", [128, 1024], F32)
    pbZ = nc.alloc_psum_tensor("pbZ", [1, 1024], F32)
    PS = [(pbS0, 0), (pbS0, 512), (pbS1, 0), (pbS1, 512)]
    PA = [(pbA, 0), (pbA, 512)]

    ps_free = {}
    buf_free = {}

    def sb(h, poff, plen, coff, dims):
        fl = h.shape[1]
        return AP(h, poff * fl + coff, [[fl, plen]] + [list(d) for d in dims])

    def psap(slot, coff, dims):
        t, base = slot
        return AP(t, base + coff, [[t.shape[1]] + [dims[0]]] + [list(d) for d in dims[1:]])

    ev_cos = p.din(sb(costok, 0, 128, 0, [[1, S]]), AP(costok_d, 0, [[S, 128], [1, S]]))
    ev_sin = p.din(sb(sintok, 0, 128, 0, [[1, S]]), AP(sintok_d, 0, [[S, 128], [1, S]]))
    ev_msk = p.din(sb(maskc, 0, 128, 0, [[1, NT]]), AP(maskc_d, 0, [[NT, 128], [1, NT]]))
    ev_idf = p.din(sb(identf, 0, 128, 0, [[1, 128]]), AP(identf_d, 0, [[128, 128], [1, 128]]))
    ev_one = p.din(sb(onesf, 0, 128, 0, [[1, 1]]), AP(onesf_d, 0, [[1, 128], [1, 1]]))
    ev_onr = p.din(sb(onesr, 0, 1, 0, [[1, 128]]), AP(onesr_d, 0, [[128, 1], [1, 128]]))
    ev_rb = p.din(sb(routerb, 0, 1, 0, [[1, 8]]), AP(routerb_d, 0, [[8, 1], [1, 8]]))
    consts = nc.alloc_sbuf_tensor("consts_sb", [128, 4], F32)
    ev_cst = p.din(sb(consts, 0, 128, 0, [[1, 4]]), AP(consts_d, 0, [[4, 128], [1, 4]]))

    ident_ap = sb(identf, 0, 128, 0, [[1, 128]])
    ones_ap = sb(onesf, 0, 128, 0, [[1, 1]])
    onesr_ap = sb(onesr, 0, 1, 0, [[1, 128]])
    cEPS = sb(consts, 0, 128, 0, [[1, 1]])
    cNHALF = sb(consts, 0, 128, 1, [[1, 1]])
    cQKS = sb(consts, 0, 128, 2, [[1, 1]])

    def ln_phase(src_dram, dst_scratch, extra_wait):
        dst_ev = [None] * NT
        for i in range(NT):
            par = i % 2
            xap = sb(xa[par], 0, 128, 0, [[1, D]])
            xrd = p.din(xap, AP(src_dram, i * 128 * D, [[D, 128], [1, D]]),
                        [buf_free.get(("xa", par)), extra_wait])
            sums = sb(st1, 0, 128, 0, [[1, 1]])
            ssq = sb(st1, 0, 128, 1, [[1, 1]])
            hfull = sb(hbuf, 0, 128, 0, [[1, D]])
            p.v_red(ALU.add, sums, xap, [xrd])
            e2a = p.s_act(hfull, xap, AF.Square,
                          waits=[xrd, buf_free.get(("hbuf",))])
            e2 = p.v_red(ALU.add, ssq, hfull, [e2a])
            m = sb(st1, 0, 128, 2, [[1, 1]])
            ex2 = sb(st1, 0, 128, 3, [[1, 1]])
            var = sb(st1, 0, 128, 4, [[1, 1]])
            lnv = sb(st1, 0, 128, 5, [[1, 1]])
            rstd = sb(st1, 0, 128, 6, [[1, 1]])
            nmr = sb(st1, 0, 128, 7, [[1, 1]])
            m2 = sb(st1, 0, 128, 8, [[1, 1]])
            m_ev = p.v_ts(m, sums, 1.0 / D, None, ALU.mult, ALU.bypass)
            p.v_ts(ex2, ssq, 1.0 / D, None, ALU.mult, ALU.bypass)
            e_m2 = p.s_act(m2, m, AF.Square, waits=[m_ev])
            e3 = p.v_tt(ALU.subtract, var, ex2, m2, waits=[e_m2])
            e4 = p.s_act(lnv, var, AF.Ln, bias=cEPS, scale=1.0,
                         waits=[e3, ev_cst])
            e5 = p.s_act(rstd, lnv, AF.Exp, scale=cNHALF)
            e6 = p.v_ts(nmr, m, rstd, -1.0, ALU.mult, ALU.mult, waits=[e5])
            e7 = p.v_ts(hfull, xap, rstd, nmr,
                        ALU.mult, ALU.add, waits=[e2], fwaits=[e6])
            buf_free[("xa", par)] = e7
            evacs = []
            for g in range(8):
                si = g % 4
                slot = PS[si]
                trev = None
                for c in range(4):
                    ch = 4 * g + c
                    w = [e7, ev_idf]
                    if c == 0:
                        w.append(ps_free.get(("PS", si)))
                    trev = p.tr(psap(slot, c * 128, [128, [1, 128]]),
                                sb(hbuf, 0, 128, ch * 128, [[1, 128]]),
                                ident_ap, inc=(c == 3), waits=w,
                                start=(c == 0), stop=(c == 3))
                w = [trev]
                if g == 0:
                    w.append(buf_free.get(("ht", par)))
                eev = p.v_copy(sb(ht[par], 0, 128, g * 512, [[1, 512]]),
                               psap(slot, 0, [128, [1, 512]]), waits=w)
                ps_free[("PS", si)] = eev
                evacs.append(eev)
            buf_free[("hbuf",)] = evacs[-1]
            dst_ev[i] = p.dout(AP(dst_scratch, i * 128 * D, [[D, 128], [1, D]]),
                               sb(ht[par], 0, 128, 0, [[1, D]]), waits=evacs)
            buf_free[("ht", par)] = dst_ev[i]
        return dst_ev

    pa_par = [0]

    def rope_evac(slot, slot_id, i, scatter_dst, mm_ev):
        t, base = slot
        fl = t.shape[1]
        ps3 = lambda off, n: AP(t, base + off, [[fl, 128], [128, 4], [1, n]])
        cos3 = lambda off, n: AP(costok, i * 128 + off, [[S, 128], [0, 4], [1, n]])
        sin3 = lambda off, n: AP(sintok, i * 128 + off, [[S, 128], [0, 4], [1, n]])
        tmpc = lambda off, n: AP(evb, off, [[2048, 128], [128, 4], [1, n]])
        tmps = lambda off, n: AP(evb, 512 + off, [[2048, 128], [128, 4], [1, n]])
        eo3 = lambda off, n: AP(evb, 1024 + off, [[2048, 128], [128, 4], [1, n]])
        w0 = [mm_ev, ev_cos, ev_sin, buf_free.get(("ev",))]
        p.v_tt(ALU.mult, tmpc(0, 128), ps3(0, 128), cos3(0, 128), waits=w0)
        p.v_tt(ALU.mult, tmps(0, 64), ps3(64, 64), sin3(0, 64))
        e3 = p.v_tt(ALU.mult, tmps(64, 64), ps3(0, 64), sin3(64, 64))
        ps_free[slot_id] = e3
        p.v_tt(ALU.subtract, eo3(0, 64), tmpc(0, 64), tmps(0, 64))
        e5 = p.v_tt(ALU.add, eo3(64, 64), tmpc(64, 64), tmps(64, 64))
        pid = ("PA", pa_par[0])
        slotA = PA[pa_par[0]]
        pa_par[0] ^= 1
        trev = None
        for c in range(4):
            w = [e5, ev_idf]
            if c == 0:
                w.append(ps_free.get(pid))
            trev = p.tr(psap(slotA, c * 128, [128, [1, 128]]),
                        AP(evb, 1024 + c * 128, [[2048, 128], [1, 128]]),
                        ident_ap, inc=(c == 3), waits=w,
                        start=(c == 0), stop=(c == 3))
        eg = p.v_copy(AP(evb, 1536, [[2048, 128], [1, 512]]),
                      psap(slotA, 0, [128, [1, 512]]),
                      waits=[trev, buf_free.get(("gather",))])
        ps_free[pid] = eg
        buf_free[("ev",)] = eg
        dev = p.dout(scatter_dst, AP(evb, 1536, [[2048, 128], [128, 4], [1, 128]]),
                     waits=[eg])
        buf_free[("gather",)] = dev
        return dev

    ht_ev = ln_phase(x_d, ht_s, None)

    qt_write_evs = []
    kvT_ready = []
    slab_free = None
    last_mm = None
    for sj in range(9):
        if sj == 0:
            srcs = [AP(wc_d, k * 8 * 128 * DL, [[DL, 128], [128 * DL, 8], [1, DL]])
                    for k in range(4)]
        else:
            srcs = [AP(wq_d, (sj - 1) * OS + k * 8 * 128 * D,
                       [[D, 128], [128 * D, 8], [1, OS]]) for k in range(4)]
        sl_evs = [p.din(sb(slab, 0, 128, k * 8 * OS, [[OS, 8], [1, OS]]),
                        srcs[k], [slab_free]) for k in range(4)]
        for i in range(NT):
            par = i % 2
            htev = p.din(sb(ht[par], 0, 128, 0, [[1, D]]),
                         AP(ht_s, i * 128 * D, [[D, 128], [1, D]]),
                         [buf_free.get(("ht", par)), ht_ev[-1]])
            si = (sj * NT + i) % 4
            slot = PS[si]
            mm_ev = None
            for c in range(ND):
                w = ([htev] + sl_evs + [ps_free.get(("PS", si))]) if c == 0 else []
                mm_ev = p.mm(psap(slot, 0, [128, [1, OS]]),
                             sb(ht[par], 0, 128, c * 128, [[1, 128]]),
                             sb(slab, 0, 128, c * OS, [[1, OS]]),
                             start=(c == 0), stop=(c == ND - 1),
                             inc=(c == ND - 1), waits=w)
            last_mm = mm_ev
            buf_free[("ht", par)] = mm_ev
            if sj == 0:
                eg0 = p.v_copy(AP(evb, 1536, [[2048, 128], [1, 512]]),
                               psap(slot, 0, [128, [1, 512]]),
                               waits=[mm_ev, buf_free.get(("gather",))])
                ps_free[("PS", si)] = eg0
                pid = ("PA", pa_par[0])
                slotA = PA[pa_par[0]]
                pa_par[0] ^= 1
                trev = None
                for c in range(4):
                    w = [eg0, ev_idf]
                    if c == 0:
                        w.append(ps_free.get(pid))
                    trev = p.tr(psap(slotA, c * 128, [128, [1, 128]]),
                                AP(evb, 1536 + c * 128, [[2048, 128], [1, 128]]),
                                ident_ap, inc=(c == 3), waits=w,
                                start=(c == 0), stop=(c == 3))
                buf_free[("gather",)] = trev
                ekv = p.v_copy(AP(kvt, i * 128, [[8192, 128], [2048, 4], [1, 128]]),
                               AP(slotA[0], slotA[1],
                                  [[slotA[0].shape[1], 128], [128, 4], [1, 128]]),
                               waits=[trev])
                ps_free[pid] = ekv
                kvT_ready.append(ekv)
            else:
                hb = 4 * (sj - 1)
                scatter = AP(qt_s, hb * 128 * S + i * 128,
                             [[S, 128], [128 * S, 4], [1, 128]])
                qt_write_evs.append(rope_evac(slot, ("PS", si), i, scatter, mm_ev))
        slab_free = last_mm

    kt_write_evs = []
    v_write_evs = []
    for sj in range(16):
        isk = sj < 8
        j = sj if isk else sj - 8
        wsrc_d = wk_d if isk else wv_d
        sl_ev = p.din(sb(slab, 0, 128, 0, [[OS, 4], [1, OS]]),
                      AP(wsrc_d, j * OS, [[D, 128], [128 * D, 4], [1, OS]]),
                      [slab_free])
        for i in range(NT):
            si = i % 4
            slot = PS[si]
            mm_ev = None
            for c in range(NL):
                w = ([sl_ev] + kvT_ready + [ps_free.get(("PS", si))]) if c == 0 else []
                mm_ev = p.mm(psap(slot, 0, [128, [1, OS]]),
                             AP(kvt, c * 2048 + i * 128, [[8192, 128], [1, 128]]),
                             sb(slab, 0, 128, c * OS, [[1, OS]]),
                             start=(c == 0), stop=(c == NL - 1),
                             inc=(c == NL - 1), waits=w)
            last_mm = mm_ev
            if isk:
                hb = 4 * j
                scatter = AP(kt_s, hb * 128 * S + i * 128,
                             [[S, 128], [128 * S, 4], [1, 128]])
                kt_write_evs.append(rope_evac(slot, ("PS", si), i, scatter, mm_ev))
            else:
                eg0 = p.v_copy(AP(evb, 1536, [[2048, 128], [1, 512]]),
                               psap(slot, 0, [128, [1, 512]]),
                               waits=[mm_ev, buf_free.get(("gather",))])
                ps_free[("PS", si)] = eg0
                dev = p.dout(AP(v_s, i * 128 * D + j * OS, [[D, 128], [1, OS]]),
                             AP(evb, 1536, [[2048, 128], [1, 512]]), waits=[eg0])
                buf_free[("gather",)] = dev
                v_write_evs.append(dev)
        slab_free = last_mm

    phaseA_end = [qt_write_evs[-1], kt_write_evs[-1], v_write_evs[-1]]
    a3_t = p.now("tensor", "st")
    buf_free[("bq", 0)] = buf_free[("bq", 1)] = a3_t
    ps_free[("psT", 0)] = ps_free[("psT", 1)] = a3_t
    buf_free[("zrep",)] = buf_free[("atev",)] = a3_t

    at_write_evs = []
    qkv_ev = {}
    for h in range(NH + 1):
        if h < NH:
            ph = h % 2
            base = ph * 6144
            w = [buf_free.get(("bq", ph))] + (phaseA_end if h < 2 else [])
            e_q = p.din(AP(slab, base, [[16384, 128], [1, S]]),
                        AP(qt_s, h * 128 * S, [[S, 128], [1, S]]), w)
            e_k = p.din(AP(slab, base + 2048, [[16384, 128], [1, S]]),
                        AP(kt_s, h * 128 * S, [[S, 128], [1, S]]), w)
            e_v = [p.din(AP(slab, base + 4096 + k * 1024,
                            [[16384, 128], [128, 8], [1, 128]]),
                         AP(v_s, h * 128 + k * 8 * 128 * D,
                            [[D, 128], [128 * D, 8], [1, 128]]), w)
                   for k in range(2)]
            qkv_ev[h] = [e_q, e_k] + e_v
        if h == 0:
            continue
        hh = h - 1
        ph = hh % 2
        base = ph * 6144
        for g in range(2):
            last_z = None
            for j in range(16):
                pj = j % 2
                pbS = pbS0 if pj == 0 else pbS1
                psT_off = pj * 1024
                wqk = qkv_ev[hh] + [ps_free.get(("BS", pj))]
                for half in range(2):
                    p.mm(AP(pbS, half * 512, [[1024, 128], [1, 512]]),
                         AP(slab, base + 2048 + j * 128, [[16384, 128], [1, 128]]),
                         AP(slab, base + g * 1024 + half * 512,
                            [[16384, 128], [1, 512]]),
                         start=True, stop=True, inc=(half == 1),
                         waits=wqk if half == 0 else [])
                e_qk = p.now("tensor", "st")
                e_ex = p.s_act(AP(kvt, psT_off, [[8192, 128], [1, 1024]]),
                               AP(pbS, 0, [[1024, 128], [1, 1024]]),
                               AF.Exp, scale=cQKS,
                               waits=[e_qk, ps_free.get(("psT", pj)), ev_cst])
                ps_free[("BS", pj)] = e_ex
                for half in range(2):
                    w = [e_ex]
                    if j == 0 and half == 0:
                        w += [ps_free.get(("pbA",)), ps_free.get(("pbZ",))]
                    p.mm(AP(pbA, half * 512, [[1024, 128], [1, 512]]),
                         AP(slab, base + 4096 + j * 128, [[16384, 128], [1, 128]]),
                         AP(kvt, psT_off + half * 512, [[8192, 128], [1, 512]]),
                         start=(j == 0), stop=(j == 15), inc=False, waits=w)
                    last_z = p.mm(AP(pbZ, half * 512, [[1024, 1], [1, 512]]),
                                  ones_ap,
                                  AP(kvt, psT_off + half * 512, [[8192, 128], [1, 512]]),
                                  start=(j == 0), stop=(j == 15), inc=(half == 1),
                                  waits=[ev_one] if (j == 0 and half == 0) else [])
                ps_free[("psT", pj)] = last_z
            e_zc = p.v_copy(AP(kvt, 4096, [[8192, 1], [1, 1024]]),
                            AP(pbZ, 0, [[1024, 1], [1, 1024]]), waits=[last_z])
            ps_free[("pbZ",)] = e_zc
            e_zr = p.v_recip(AP(kvt, 5120, [[8192, 1], [1, 1024]]),
                             AP(kvt, 4096, [[8192, 1], [1, 1024]]))
            bc_ev = None
            for half in range(2):
                w = ([e_zr, ev_onr, ps_free.get(("BS", 0))]
                     if half == 0 else [])
                bc_ev = p.mm(AP(pbS0, half * 512, [[1024, 128], [1, 512]]),
                             onesr_ap,
                             AP(kvt, 5120 + half * 512, [[8192, 1], [1, 512]]),
                             start=True, stop=True, inc=(half == 1), waits=w)
            e_zv = p.v_copy(AP(kvt, 2048, [[8192, 128], [1, 1024]]),
                            AP(pbS0, 0, [[1024, 128], [1, 1024]]),
                            waits=[bc_ev, buf_free.get(("zrep",))])
            ps_free[("BS", 0)] = e_zv
            e_at = p.v_tt(ALU.mult, AP(kvt, 3072, [[8192, 128], [1, 1024]]),
                          AP(pbA, 0, [[1024, 128], [1, 1024]]),
                          AP(kvt, 2048, [[8192, 128], [1, 1024]]),
                          waits=[e_zv, buf_free.get(("atev",)), last_z])
            ps_free[("pbA",)] = e_at
            buf_free[("zrep",)] = e_at
            e_ad = p.dout(AP(at_s, hh * 128 * S + g * 1024, [[S, 128], [1, 1024]]),
                          AP(kvt, 3072, [[8192, 128], [1, 1024]]), waits=[e_at])
            buf_free[("atev",)] = e_ad
            at_write_evs.append(e_ad)
        buf_free[("bq", ph)] = p.now("tensor", "st")

    b_end_v = p.now("vector", "sv")
    b_end_out = p.now("gpsimd", "sout")
    buf_free[("xsl", 0)] = buf_free[("xsl", 1)] = b_end_v
    buf_free[("eo",)] = b_end_v
    buf_free[("ev",)] = b_end_v
    buf_free[("gather",)] = b_end_v

    o1_tile_ev = {}
    slab_free = p.now("tensor", "st")
    for sj in range(8):
        sl_evs = [p.din(sb(slab, 0, 128, k * 8 * OS, [[OS, 8], [1, OS]]),
                        AP(wo_d, sj * OS + k * 8 * 128 * D,
                           [[D, 128], [128 * D, 8], [1, OS]]),
                        [slab_free]) for k in range(4)]
        for i in range(NT):
            par = i % 2
            w = [buf_free.get(("ht", par))] + (at_write_evs if (sj == 0 and i == 0) else [])
            a_evs = [p.din(sb(ht[par], 0, 128, k * 1024, [[128, 8], [1, 128]]),
                           AP(at_s, (8 * k) * 128 * S + i * 128,
                              [[S, 128], [128 * S, 8], [1, 128]]), w)
                     for k in range(4)]
            exs = p.din(AP(evb, par * 512, [[2048, 128], [1, 512]]),
                        AP(x_d, i * 128 * D + sj * OS, [[D, 128], [1, OS]]),
                        [buf_free.get(("xsl", par))])
            si = i % 4
            slot = PS[si]
            mm_ev = None
            for c in range(ND):
                w = (a_evs + sl_evs + [ps_free.get(("PS", si))]) if c == 0 else []
                mm_ev = p.mm(psap(slot, 0, [128, [1, OS]]),
                             sb(ht[par], 0, 128, c * 128, [[1, 128]]),
                             sb(slab, 0, 128, c * OS, [[1, OS]]),
                             start=(c == 0), stop=(c == ND - 1),
                             inc=(c == ND - 1), waits=w)
            last_mm = mm_ev
            buf_free[("ht", par)] = mm_ev
            e_o1 = p.v_tt(ALU.add, AP(evb, 1024, [[2048, 128], [1, 512]]),
                          psap(slot, 0, [128, [1, 512]]),
                          AP(evb, par * 512, [[2048, 128], [1, 512]]),
                          waits=[mm_ev, exs, buf_free.get(("eo",))])
            ps_free[("PS", si)] = e_o1
            buf_free[("xsl", par)] = e_o1
            dev = p.dout(AP(o1_s, i * 128 * D + sj * OS, [[D, 128], [1, OS]]),
                         AP(evb, 1024, [[2048, 128], [1, 512]]), waits=[e_o1])
            buf_free[("eo",)] = dev
            o1_tile_ev[(sj, i)] = dev
        slab_free = last_mm

    c1_last = o1_tile_ev[(7, NT - 1)]

    lt_ev = ln_phase(o1_s, lt_s, c1_last)

    e_a2 = [p.din(sb(slab, 0, 128, k * 8 * 128, [[128, 8], [1, 128]]),
                  AP(a2t_d, k * 8 * 128 * 128, [[128, 128], [128 * 128, 8], [1, 128]]),
                  [slab_free]) for k in range(4)]
    e_rw = p.din(sb(slab, 0, 128, 4096, [[8, 32], [1, 8]]),
                 AP(rwt_d, 0, [[8, 128], [128 * 8, 32], [1, 8]]), [slab_free])
    t_evs = []
    rl_evs = []
    mm_r = None
    for i in range(NT):
        par = i % 2
        htev = p.din(sb(ht[par], 0, 128, 0, [[1, D]]),
                     AP(lt_s, i * 128 * D, [[D, 128], [1, D]]),
                     [buf_free.get(("ht", par)), lt_ev[-1]])
        mm_t = None
        for c in range(ND):
            w = ([htev] + e_a2 + [e_rw, ps_free.get(("PS", 0))]) if c == 0 else []
            w2 = [ps_free.get(("PS", 2))] if c == 0 else []
            mm_t = p.mm(AP(pbS0, 0, [[1024, 128], [1, 128]]),
                        sb(ht[par], 0, 128, c * 128, [[1, 128]]),
                        sb(slab, 0, 128, c * 128, [[1, 128]]),
                        start=(c == 0), stop=(c == ND - 1),
                        inc=(c == ND - 1), waits=w)
            mm_r = p.mm(AP(pbS1, 0, [[1024, 128], [1, 8]]),
                        sb(ht[par], 0, 128, c * 128, [[1, 128]]),
                        sb(slab, 0, 128, 4096 + c * 8, [[1, 8]]),
                        start=(c == 0), stop=(c == ND - 1),
                        inc=(c == ND - 1), waits=w2)
        buf_free[("ht", par)] = mm_r
        e_t = p.v_copy(AP(kvt, i * 128, [[8192, 128], [1, 128]]),
                       AP(pbS0, 0, [[1024, 128], [1, 128]]), waits=[mm_t])
        ps_free[("PS", 0)] = e_t
        e_rl = p.v_copy(AP(evb, i * 8, [[2048, 128], [1, 8]]),
                        AP(pbS1, 0, [[1024, 128], [1, 8]]), waits=[mm_r])
        ps_free[("PS", 2)] = e_rl
        t_evs.append(e_t)
        rl_evs.append(e_rl)
    lg_mm = None
    for i in range(NT):
        w = [rl_evs[i], ev_msk] + ([ps_free.get(("pbZ",))] if i == 0 else [])
        lg_mm = p.mm(AP(pbZ, 0, [[1024, 1], [1, 8]]),
                     sb(maskc, 0, 128, i, [[1, 1]]),
                     AP(evb, i * 8, [[2048, 128], [1, 8]]),
                     start=(i == 0), stop=(i == NT - 1),
                     inc=(i == NT - 1), waits=w)
    lg = AP(smt, 0, [[64, 1], [1, 8]])
    mx = AP(smt, 8, [[64, 1], [1, 1]])
    nmx = AP(smt, 9, [[64, 1], [1, 1]])
    pr = AP(smt, 16, [[64, 1], [1, 8]])
    sm1 = AP(smt, 24, [[64, 1], [1, 1]])
    rsm = AP(smt, 25, [[64, 1], [1, 1]])
    rout = AP(smt, 32, [[64, 1], [1, 8]])
    e1 = p.v_tt(ALU.add, lg, AP(pbZ, 0, [[1024, 1], [1, 8]]),
                AP(routerb, 0, [[8, 1], [1, 8]]), waits=[lg_mm, ev_rb])
    ps_free[("pbZ",)] = e1
    p.v_red(ALU.max, mx, lg)
    p.v_ts(nmx, mx, -1.0, None, ALU.mult, ALU.bypass)
    e2 = p.s_act(pr, lg, AF.Exp, bias=nmx, scale=1.0, accum=sm1,
                 waits=[p.now("vector", "sv")])
    e3r = p.v_recip(rsm, sm1, waits=[e2])
    e4 = p.v_ts(rout, pr, rsm, 2.0, ALU.mult, ALU.mult, waits=[e2],
                fwaits=[e3r])
    e5 = p.mm(AP(pbS0, 0, [[1024, 128], [1, 8]]), onesr_ap, rout,
              start=True, stop=True, inc=True,
              waits=[e4, ev_onr, ps_free.get(("PS", 0))])
    e6 = p.v_copy(AP(st2, 0, [[8, 128], [1, 8]]),
                  AP(pbS0, 0, [[1024, 128], [1, 8]]), waits=[e5])
    ps_free[("PS", 0)] = e6
    e7 = None
    for ee in range(8):
        e7 = p.v_ts(AP(evb, ee * 16, [[2048, 128], [128, 16], [1, 16]]),
                    AP(kvt, ee * 16, [[8192, 128], [128, 16], [1, 16]]),
                    sb(st2, 0, 128, ee, [[1, 1]]), None, ALU.mult, ALU.bypass,
                    waits=t_evs if ee == 0 else [],
                    fwaits=[e6] if ee == 0 else [])
    ts_evs = []
    for gg in range(4):
        pid = ("PA", gg % 2)
        slotA = PA[gg % 2]
        trev = None
        for c in range(4):
            i = 4 * gg + c
            w = [e7, ev_idf] + ([ps_free.get(pid)] if c == 0 else [])
            trev = p.tr(psap(slotA, c * 128, [128, [1, 128]]),
                        AP(evb, i * 128, [[2048, 128], [1, 128]]),
                        ident_ap, inc=(c == 3), waits=w,
                        start=(c == 0), stop=(c == 3))
        e8 = p.v_copy(AP(kvt, 2048 + gg * 512, [[8192, 128], [1, 512]]),
                      psap(slotA, 0, [128, [1, 512]]), waits=[trev])
        ps_free[pid] = e8
        ts_evs.append(e8)
    if dbg_d is not None:
        p.dout(AP(dbg_d, 0, [[8192, 128], [1, 2048]]),
               AP(evb, 0, [[2048, 128], [1, 2048]]), waits=[ts_evs[-1]])
        p.dout(AP(dbg_d, 2048, [[8192, 128], [1, 2048]]),
               AP(kvt, 2048, [[8192, 128], [1, 2048]]), waits=[ts_evs[-1]])
        p.dout(AP(dbg_d, 4096, [[8192, 128], [1, 2048]]),
               AP(kvt, 0, [[8192, 128], [1, 2048]]), waits=[ts_evs[-1]])
        p.dout(AP(dbg_d, 6144, [[8192, 128], [1, 8]]),
               AP(st2, 0, [[8, 128], [1, 8]]), waits=[ts_evs[-1], e6])
    slab_free = mm_r
    c2b_t = p.now("tensor", "st")
    buf_free[("brep",)] = buf_free[("tmp",)] = buf_free[("eo2",)] = c2b_t
    buf_free[("xo1", 0)] = buf_free[("xo1", 1)] = c2b_t

    e_b2 = p.din(AP(kvt, 4096, [[8192, 128], [1, D]]),
                 AP(b2_d, 0, [[D, 128], [1, D]]), ts_evs + [b_end_out])
    out_evs = []
    e_t1 = None
    for sj in range(8):
        sl_evs = [p.din(sb(slab, 0, 128, k * 8 * OS, [[OS, 8], [1, OS]]),
                        AP(wf_d, sj * OS + k * 8 * 128 * D,
                           [[D, 128], [128 * D, 8], [1, OS]]),
                        [slab_free]) for k in range(4)]
        e_br = p.din(AP(hbuf, 1024, [[D, 128], [1, 512]]),
                     AP(brep_d, sj * OS, [[D, 128], [1, OS]]),
                     [buf_free.get(("brep",))])
        for i in range(NT):
            par = i % 2
            htev = p.din(sb(ht[par], 0, 128, 0, [[1, D]]),
                         AP(lt_s, i * 128 * D, [[D, 128], [1, D]]),
                         [buf_free.get(("ht", par)), lt_ev[-1]])
            exs = p.din(AP(hbuf, par * 512, [[D, 128], [1, 512]]),
                        AP(o1_s, i * 128 * D + sj * OS, [[D, 128], [1, OS]]),
                        [buf_free.get(("xo1", par)), o1_tile_ev[(sj, i)]])
            si = i % 4
            slot = PS[si]
            mm_ev = None
            for c in range(ND):
                w = ([htev] + sl_evs + [ps_free.get(("PS", si))]) if c == 0 else []
                mm_ev = p.mm(psap(slot, 0, [128, [1, OS]]),
                             sb(ht[par], 0, 128, c * 128, [[1, 128]]),
                             sb(slab, 0, 128, c * OS, [[1, OS]]),
                             start=(c == 0), stop=False, inc=False, waits=w)
            mm_ev = p.mm(psap(slot, 0, [128, [1, OS]]),
                         AP(kvt, 2048 + i * 128, [[8192, 128], [1, 128]]),
                         AP(kvt, 4096 + sj * OS, [[8192, 128], [1, OS]]),
                         start=False, stop=True, inc=True, waits=[e_b2])
            last_mm = mm_ev
            buf_free[("ht", par)] = mm_ev
            e_t1 = p.v_tt(ALU.add, AP(hbuf, 1536, [[D, 128], [1, 512]]),
                          psap(slot, 0, [128, [1, 512]]),
                          AP(hbuf, 1024, [[D, 128], [1, 512]]),
                          waits=[mm_ev, e_br, buf_free.get(("tmp",))])
            ps_free[("PS", si)] = e_t1
            e_t2 = p.v_tt(ALU.add, AP(hbuf, 2048, [[D, 128], [1, 512]]),
                          AP(hbuf, 1536, [[D, 128], [1, 512]]),
                          AP(hbuf, par * 512, [[D, 128], [1, 512]]),
                          waits=[exs, buf_free.get(("eo2",))])
            buf_free[("tmp",)] = e_t2
            buf_free[("xo1", par)] = e_t2
            dev = p.dout(AP(out_d, i * 128 * D + sj * OS, [[D, 128], [1, OS]]),
                         AP(hbuf, 2048, [[D, 128], [1, 512]]), waits=[e_t2])
            buf_free[("eo2",)] = dev
            out_evs.append(dev)
        slab_free = last_mm
        buf_free[("brep",)] = e_t1

    p.ops["gpsimd"].append(("w", "sout", p.cnt["sout"]))

    with (
        nc.semaphore("sin") as h_sin,
        nc.semaphore("sout") as h_sout,
        nc.semaphore("st") as h_st,
        nc.semaphore("sv") as h_sv,
        nc.semaphore("ss") as h_ss,
        nc.Block() as block,
    ):
        sems = {"sin": h_sin, "sout": h_sout, "st": h_st, "sv": h_sv, "ss": h_ss}

        def mkbody(eng_ops):
            def body(eng):
                for op in eng_ops:
                    if op[0] == "w":
                        eng.wait_ge(sems[op[1]], op[2])
                    else:
                        inst = op[1](eng)
                        for (s, k) in op[2]:
                            inst.then_inc(sems[s], k)
            return body

        block.tensor(mkbody(p.ops["tensor"]))
        block.vector(mkbody(p.ops["vector"]))
        block.scalar(mkbody(p.ops["scalar"]))
        block.sync(mkbody(p.ops["sync"]))
        block.gpsimd(mkbody(p.ops["gpsimd"]))

    return nc


def _host_inputs(inputs):
    f32 = np.float32
    Wq = np.ascontiguousarray(inputs["L0_attn_Wq"], f32)
    Wc = np.ascontiguousarray(inputs["L0_attn_Wc"], f32)
    Wk = np.ascontiguousarray(inputs["L0_attn_Wk"], f32)
    Wv = np.ascontiguousarray(inputs["L0_attn_Wv"], f32)
    Wo = np.ascontiguousarray(inputs["L0_attn_Wo"], f32)
    Wf = np.ascontiguousarray(np.asarray(inputs["L0_ffn_W"]).T, f32)
    A = np.asarray(inputs["L0_ffn_A"], f32)
    B = np.asarray(inputs["L0_ffn_B"], f32)
    A2T = np.ascontiguousarray(A.reshape(128, D).T, f32)
    B2 = np.ascontiguousarray(np.transpose(B, (0, 2, 1)).reshape(128, D), f32)
    RWT = np.ascontiguousarray(np.asarray(inputs["L0_ffn_router_W"]).T, f32)
    brep = np.ascontiguousarray(
        np.broadcast_to(np.asarray(inputs["L0_ffn_b"], f32), (128, D)), f32)
    routerb = np.ascontiguousarray(
        np.asarray(inputs["L0_ffn_router_b"], f32).reshape(1, 8), f32)

    pos = np.arange(S, dtype=np.float64)
    inv = 1.0 / (10000.0 ** (np.arange(0, 128, 2, dtype=np.float64) / 128.0))
    ang = pos[:, None] * inv[None, :]
    ang = np.concatenate([ang, ang], axis=1)
    cos = np.cos(ang).astype(f32)
    sin = np.sin(ang).astype(f32)
    costok = np.zeros((128, S), f32)
    sintok = np.zeros((128, S), f32)
    for i in range(NT):
        costok[:, i * 128:(i + 1) * 128] = cos[i * 128:(i + 1) * 128, :]
        sintok[:, i * 128:(i + 1) * 128] = sin[i * 128:(i + 1) * 128, :]

    maskc = np.zeros((128, NT), f32)
    for t in range(611, 2047):
        maskc[t % 128, t // 128] = np.float32(np.float32(1.0) / np.float32(1436.0))

    identf = np.eye(128, dtype=f32)
    onesf = np.ones((128, 1), f32)
    onesr = np.ones((1, 128), f32)
    consts = np.zeros((128, 4), f32)
    consts[:, 0] = NEPS
    consts[:, 1] = -0.5
    consts[:, 2] = QK_SCALE

    return dict(wq=Wq, wc=Wc, wk=Wk, wv=Wv, wo=Wo, wf=Wf, a2t=A2T, b2=B2,
                rwt=RWT, brep=brep, routerb=routerb, costok=costok,
                sintok=sintok, maskc=maskc, identf=identf, onesf=onesf,
                onesr=onesr, consts=consts)


def kernel(**inputs):
    xfull = np.ascontiguousarray(inputs["L0_attn_hidden_states"], np.float32)
    com = _host_inputs(inputs)
    nc = build_nc()
    in_maps = [dict(com, x=xfull[b]) for b in range(8)]
    res = run_bass_kernel_spmd(nc, in_maps, core_ids=list(range(8)))
    return np.stack([res.results[b]["out"] for b in range(8)], axis=0)


# version 05
# speedup vs baseline: 1.1898x; 1.0397x over previous
import numpy as np
import concourse.bass as bass
import concourse.mybir as mybir
from concourse.bass_utils import run_bass_kernel_spmd

F32 = mybir.dt.float32
BF16 = mybir.dt.bfloat16
AF = mybir.ActivationFunctionType
ALU = mybir.AluOpType
AX = mybir.AxisListType

S, D, DL = 2048, 4096, 512
NT, ND, NL, NH = 16, 32, 4, 32
OS = 512
NEPS = 1e-5
QK_SCALE = float(1.0 / np.sqrt(128.0))
PASSES = ((0, 0), (0, 1), (1, 0))

ENGINES = ("tensor", "vector", "scalar", "sync", "gpsimd")
DMA_SEMS = ("sin", "sout")


def AP(h, off, dims):
    return bass.AP(h, off, [list(d) for d in dims])


class Plan:
    def __init__(self):
        self.ops = {e: [] for e in ENGINES}
        self.cnt = {"sin": 0, "sout": 0, "st": 0, "sv": 0, "ss": 0}

    def now(self, eng, sem):
        return (eng, sem, self.cnt[sem])

    def emit(self, eng, fn, incs=(), waits=(), fwaits=()):
        lst = self.ops[eng]
        merged = {}
        for w in waits:
            if w is None:
                continue
            weng, wsem, wval = w
            if weng == eng and wsem not in DMA_SEMS:
                continue
            if wval <= 0:
                continue
            merged[wsem] = max(merged.get(wsem, 0), wval)
        for w in fwaits:
            if w is None:
                continue
            weng, wsem, wval = w
            if wval <= 0:
                continue
            merged[wsem] = max(merged.get(wsem, 0), wval)
        for wsem, wval in merged.items():
            lst.append(("w", wsem, wval))
        lst.append(("i", fn, tuple(incs)))
        ev = None
        for (s, k) in incs:
            self.cnt[s] += k
            ev = (eng, s, self.cnt[s])
        return ev

    def din(self, dst, src, waits=(), fence=True):
        ev = self.emit("sync",
                       lambda e, d=dst, s=src: e.dma_start(out=d, in_=s),
                       [("sin", 16)], waits)
        if fence:
            self.ops["sync"].append(("w", "sin", ev[2]))
        return ev

    def dout(self, dst, src, waits=()):
        return self.emit("gpsimd",
                         lambda e, d=dst, s=src: e.dma_start(out=d, in_=s),
                         [("sout", 16)], waits)

    def mm(self, out, lhsT, rhs, start, stop, inc=False, waits=()):
        return self.emit(
            "tensor",
            lambda e, o=out, l=lhsT, r=rhs, a=start, b=stop:
                e.matmul(o, l, r, start=a, stop=b),
            [("st", 1)] if inc else [], waits)

    def tr(self, out, in_, ident, inc=False, waits=(), start=True, stop=True):
        return self.emit(
            "tensor",
            lambda e, o=out, i=in_, d=ident, a=start, b=stop:
                e.matmul(o, i, d, start=a, stop=b, is_transpose=True),
            [("st", 1)] if inc else [], waits)

    def v_tt(self, op, out, a, b, waits=()):
        return self.emit(
            "vector",
            lambda e, o=out, x=a, y=b, p_=op: e.tensor_tensor(o, x, y, p_),
            [("sv", 1)], waits)

    def v_ts(self, out, in0, s1, s2, op0, op1, waits=(), fwaits=()):
        if s2 is None:
            return self.emit(
                "vector",
                lambda e, o=out, x=in0, a=s1, p0=op0:
                    e.tensor_single_scalar(o, x, a, p0),
                [("sv", 1)], waits, fwaits)
        return self.emit(
            "vector",
            lambda e, o=out, x=in0, a=s1, b=s2, p0=op0, p1=op1:
                e.tensor_scalar(o, x, a, b, p0, p1),
            [("sv", 1)], waits, fwaits)

    def v_red(self, op, out, in_, waits=()):
        return self.emit(
            "vector",
            lambda e, o=out, x=in_, p_=op: e.tensor_reduce(o, x, AX.X, p_),
            [("sv", 1)], waits)

    def v_copy(self, out, in_, waits=()):
        return self.emit(
            "vector",
            lambda e, o=out, x=in_: e.tensor_copy(o, x),
            [("sv", 1)], waits)

    def v_recip(self, out, in_, waits=()):
        return self.emit(
            "vector",
            lambda e, o=out, x=in_: e.reciprocal(o, x),
            [("sv", 1)], waits)

    def s_act(self, out, in_, func, bias=0.0, scale=1.0, accum=None, waits=()):
        return self.emit(
            "scalar",
            lambda e, o=out, x=in_, f=func, b=bias, s=scale, ac=accum:
                e.activation(o, x, f, bias=b, scale=s, accum_out=ac),
            [("ss", 1)], waits)


def build_nc():
    nc = bass.Bass("TRN2", target_bir_lowering=False)
    p = Plan()

    x_d = nc.dram_tensor("x", [S, D], F32, kind="ExternalInput")
    wq_d = nc.dram_tensor("wq", [D, D], F32, kind="ExternalInput")
    wc_d = nc.dram_tensor("wc", [D, DL], F32, kind="ExternalInput")
    wk_d = nc.dram_tensor("wk", [DL, D], F32, kind="ExternalInput")
    wv_d = nc.dram_tensor("wv", [DL, D], F32, kind="ExternalInput")
    wo_d = nc.dram_tensor("wo", [D, D], F32, kind="ExternalInput")
    wf_d = nc.dram_tensor("wf", [D, D], F32, kind="ExternalInput")
    a2t_d = nc.dram_tensor("a2t", [D, 128], F32, kind="ExternalInput")
    b2_d = nc.dram_tensor("b2", [128, D], F32, kind="ExternalInput")
    rwt_d = nc.dram_tensor("rwt", [D, 8], F32, kind="ExternalInput")
    brep_d = nc.dram_tensor("brep", [128, D], F32, kind="ExternalInput")
    costok_d = nc.dram_tensor("costok", [128, S], F32, kind="ExternalInput")
    sintok_d = nc.dram_tensor("sintok", [128, S], F32, kind="ExternalInput")
    maskc_d = nc.dram_tensor("maskc", [128, NT], F32, kind="ExternalInput")
    identf_d = nc.dram_tensor("identf", [128, 128], F32, kind="ExternalInput")
    onesb_d = nc.dram_tensor("onesb", [128, 1], BF16, kind="ExternalInput")
    onesr_d = nc.dram_tensor("onesr", [1, 128], F32, kind="ExternalInput")
    routerb_d = nc.dram_tensor("routerb", [1, 8], F32, kind="ExternalInput")
    consts_d = nc.dram_tensor("consts", [128, 4], F32, kind="ExternalInput")
    out_d = nc.dram_tensor("out", [S, D], F32, kind="ExternalOutput")

    SK = globals().get("SCRATCH_KIND", "Internal")
    ht_s = nc.dram_tensor("ht_s", [NT, 128, 2 * D], BF16, kind=SK)
    qt_s = nc.dram_tensor("qt_s", [NH, 128, 2 * S], BF16, kind=SK)
    kt_s = nc.dram_tensor("kt_s", [NH, 128, 2 * S], BF16, kind=SK)
    v_s = nc.dram_tensor("v_s", [NT, 128, 2 * D], BF16, kind=SK)
    at_s = nc.dram_tensor("at_s", [NH, 128, 2 * S], BF16, kind=SK)
    o1_s = nc.dram_tensor("o1_s", [S, D], F32, kind=SK)
    lt_s = nc.dram_tensor("lt_s", [NT, 128, 2 * D], BF16, kind=SK)

    costok = nc.alloc_sbuf_tensor("costok_sb", [128, S], F32)
    sintok = nc.alloc_sbuf_tensor("sintok_sb", [128, S], F32)
    maskc = nc.alloc_sbuf_tensor("maskc_sb", [128, NT], F32)
    identf = nc.alloc_sbuf_tensor("identf_sb", [128, 128], F32)
    onesb = nc.alloc_sbuf_tensor("onesb_sb", [128, 1], BF16)
    onesr = nc.alloc_sbuf_tensor("onesr_sb", [1, 128], F32)
    routerb = nc.alloc_sbuf_tensor("routerb_sb", [1, 8], F32)
    consts = nc.alloc_sbuf_tensor("consts_sb", [128, 4], F32)
    xa = [nc.alloc_sbuf_tensor(f"xa{i}", [128, D], F32) for i in range(2)]
    hbuf = nc.alloc_sbuf_tensor("hbuf", [128, D], F32)
    ht = [nc.alloc_sbuf_tensor(f"ht{i}", [128, 2 * D], BF16) for i in range(2)]
    slab = nc.alloc_sbuf_tensor("slab", [128, 32768], BF16)
    kvt = nc.alloc_sbuf_tensor("kvt", [128, 16384], BF16)
    evb = nc.alloc_sbuf_tensor("evb", [128, 2048], F32)
    ev2 = nc.alloc_sbuf_tensor("ev2", [128, 2048], BF16)
    st1 = nc.alloc_sbuf_tensor("st1", [128, 16], F32)
    st2 = nc.alloc_sbuf_tensor("st2", [128, 8], F32)
    smt = nc.alloc_sbuf_tensor("smt", [1, 64], F32)

    pbS0 = nc.alloc_psum_tensor("pbS0", [128, 1024], F32)
    pbS1 = nc.alloc_psum_tensor("pbS1", [128, 1024], F32)
    pbA = nc.alloc_psum_tensor("pbA", [128, 1024], F32)
    pbZ = nc.alloc_psum_tensor("pbZ", [1, 1024], F32)
    PS = [(pbS0, 0), (pbS0, 512), (pbS1, 0), (pbS1, 512)]
    PA = [(pbA, 0), (pbA, 512)]

    ps_free = {}
    buf_free = {}

    def sb(h, poff, plen, coff, dims):
        fl = h.shape[1]
        return AP(h, poff * fl + coff, [[fl, plen]] + [list(d) for d in dims])

    def psap(slot, coff, dims):
        t, base = slot
        return AP(t, base + coff,
                  [[t.shape[1]] + [dims[0]]] + [list(d) for d in dims[1:]])

    ev_cos = p.din(sb(costok, 0, 128, 0, [[1, S]]), AP(costok_d, 0, [[S, 128], [1, S]]))
    ev_sin = p.din(sb(sintok, 0, 128, 0, [[1, S]]), AP(sintok_d, 0, [[S, 128], [1, S]]))
    ev_msk = p.din(sb(maskc, 0, 128, 0, [[1, NT]]), AP(maskc_d, 0, [[NT, 128], [1, NT]]))
    ev_idb = p.din(sb(identf, 0, 128, 0, [[1, 128]]), AP(identf_d, 0, [[128, 128], [1, 128]]))
    ev_onb = p.din(sb(onesb, 0, 128, 0, [[1, 1]]), AP(onesb_d, 0, [[1, 128], [1, 1]]))
    ev_onr = p.din(sb(onesr, 0, 1, 0, [[1, 128]]), AP(onesr_d, 0, [[128, 1], [1, 128]]))
    ev_rb = p.din(sb(routerb, 0, 1, 0, [[1, 8]]), AP(routerb_d, 0, [[8, 1], [1, 8]]))
    ev_cst = p.din(sb(consts, 0, 128, 0, [[1, 4]]), AP(consts_d, 0, [[4, 128], [1, 4]]))

    ident_ap = sb(identf, 0, 128, 0, [[1, 128]])
    onesb_ap = sb(onesb, 0, 128, 0, [[1, 1]])
    onesr_ap = sb(onesr, 0, 1, 0, [[1, 128]])
    cEPS = sb(consts, 0, 128, 0, [[1, 1]])
    cNHALF = sb(consts, 0, 128, 1, [[1, 1]])
    cQKS = sb(consts, 0, 128, 2, [[1, 1]])

    def split(src_f32, dst_hi, dst_lo, waits=()):
        e1 = p.v_copy(dst_hi, src_f32, waits=waits)
        e2 = p.v_tt(ALU.subtract, dst_lo, src_f32, dst_hi)
        return e1, e2

    def load_slab_split(w_dram, row_len, col_off, nchunks, ow, slab_coff,
                        waits):
        evs = []
        per_stage = max(1, 4096 // ow)
        for s0 in range(0, nchunks, per_stage):
            n = min(per_stage, nchunks - s0)
            par = (s0 // per_stage) % 2
            w = [buf_free.get(("xa", par))] + (list(waits) if s0 == 0 else [])
            src = AP(w_dram, col_off + s0 * 128 * row_len,
                     [[row_len, 128], [128 * row_len, n], [1, ow]])
            exv = p.din(sb(xa[par], 0, 128, 0, [[ow, n], [1, ow]]), src, w)
            ehi = p.v_copy(sb(slab, 0, 128, slab_coff + s0 * ow, [[1, n * ow]]),
                           sb(xa[par], 0, 128, 0, [[1, n * ow]]),
                           waits=[exv] + list(waits))
            elo = p.v_tt(ALU.subtract,
                         sb(slab, 0, 128, 16384 + slab_coff + s0 * ow, [[1, n * ow]]),
                         sb(xa[par], 0, 128, 0, [[1, n * ow]]),
                         sb(slab, 0, 128, slab_coff + s0 * ow, [[1, n * ow]]))
            buf_free[("xa", par)] = elo
            evs.append(elo)
        return evs

    def ln_phase(src_dram, dst_scratch, extra_wait):
        dst_ev = [None] * NT
        for i in range(NT):
            par = i % 2
            xap = sb(xa[par], 0, 128, 0, [[1, D]])
            xrd = p.din(xap, AP(src_dram, i * 128 * D, [[D, 128], [1, D]]),
                        [buf_free.get(("xa", par)), extra_wait])
            sums = sb(st1, 0, 128, 0, [[1, 1]])
            ssq = sb(st1, 0, 128, 1, [[1, 1]])
            hfull = sb(hbuf, 0, 128, 0, [[1, D]])
            p.v_red(ALU.add, sums, xap, [xrd])
            e2a = p.s_act(hfull, xap, AF.Square,
                          waits=[xrd, buf_free.get(("hbuf",))])
            e2 = p.v_red(ALU.add, ssq, hfull, [e2a])
            m = sb(st1, 0, 128, 2, [[1, 1]])
            ex2 = sb(st1, 0, 128, 3, [[1, 1]])
            var = sb(st1, 0, 128, 4, [[1, 1]])
            lnv = sb(st1, 0, 128, 5, [[1, 1]])
            rstd = sb(st1, 0, 128, 6, [[1, 1]])
            nmr = sb(st1, 0, 128, 7, [[1, 1]])
            m2 = sb(st1, 0, 128, 8, [[1, 1]])
            m_ev = p.v_ts(m, sums, 1.0 / D, None, ALU.mult, ALU.bypass)
            p.v_ts(ex2, ssq, 1.0 / D, None, ALU.mult, ALU.bypass)
            e_m2 = p.s_act(m2, m, AF.Square, waits=[m_ev])
            e3 = p.v_tt(ALU.subtract, var, ex2, m2, waits=[e_m2])
            e4 = p.s_act(lnv, var, AF.Ln, bias=cEPS, scale=1.0,
                         waits=[e3, ev_cst])
            e5 = p.s_act(rstd, lnv, AF.Exp, scale=cNHALF)
            e6 = p.v_ts(nmr, m, rstd, -1.0, ALU.mult, ALU.mult, waits=[e5])
            e7 = p.v_ts(hfull, xap, rstd, nmr,
                        ALU.mult, ALU.add, waits=[e2], fwaits=[e6])
            buf_free[("xa", par)] = e7
            evacs = []
            for g in range(8):
                si = g % 4
                slot = PS[si]
                trev = None
                for c in range(4):
                    w = [e7, ev_idb]
                    if c == 0:
                        w.append(ps_free.get(("PS", si)))
                    trev = p.tr(psap(slot, c * 128, [128, [1, 128]]),
                                sb(hbuf, 0, 128, g * 512 + c * 128, [[1, 128]]),
                                ident_ap, inc=(c == 3), waits=w,
                                start=(c == 0), stop=(c == 3))
                w = [trev]
                if g == 0:
                    w.append(buf_free.get(("ht", par)))
                ehi = p.v_copy(sb(ht[par], 0, 128, g * 512, [[1, 512]]),
                               psap(slot, 0, [128, [1, 512]]), waits=w)
                elo = p.v_tt(ALU.subtract,
                             sb(ht[par], 0, 128, D + g * 512, [[1, 512]]),
                             psap(slot, 0, [128, [1, 512]]),
                             sb(ht[par], 0, 128, g * 512, [[1, 512]]))
                ps_free[("PS", si)] = elo
                evacs.append(elo)
            buf_free[("hbuf",)] = evacs[-1]
            dst_ev[i] = p.dout(AP(dst_scratch, i * 128 * 2 * D, [[2 * D, 128], [1, 2 * D]]),
                               sb(ht[par], 0, 128, 0, [[1, 2 * D]]), waits=evacs)
            buf_free[("ht", par)] = dst_ev[i]
        return dst_ev

    pa_par = [0]

    def rope_evac(slot, slot_id, i, scat_hi, scat_lo, mm_ev):
        t, base = slot
        fl = t.shape[1]
        ps3 = lambda off, n: AP(t, base + off, [[fl, 128], [128, 4], [1, n]])
        cos3 = lambda off, n: AP(costok, i * 128 + off, [[S, 128], [0, 4], [1, n]])
        sin3 = lambda off, n: AP(sintok, i * 128 + off, [[S, 128], [0, 4], [1, n]])
        tmpc = lambda off, n: AP(evb, off, [[2048, 128], [128, 4], [1, n]])
        tmps = lambda off, n: AP(evb, 512 + off, [[2048, 128], [128, 4], [1, n]])
        eo3 = lambda off, n: AP(evb, 1024 + off, [[2048, 128], [128, 4], [1, n]])
        w0 = [mm_ev, ev_cos, ev_sin, buf_free.get(("ev",))]
        p.v_tt(ALU.mult, tmpc(0, 128), ps3(0, 128), cos3(0, 128), waits=w0)
        p.v_tt(ALU.mult, tmps(0, 64), ps3(64, 64), sin3(0, 64))
        e3 = p.v_tt(ALU.mult, tmps(64, 64), ps3(0, 64), sin3(64, 64))
        ps_free[slot_id] = e3
        p.v_tt(ALU.subtract, eo3(0, 64), tmpc(0, 64), tmps(0, 64))
        e5 = p.v_tt(ALU.add, eo3(64, 64), tmpc(64, 64), tmps(64, 64))
        pid = ("PA", pa_par[0])
        slotA = PA[pa_par[0]]
        pa_par[0] ^= 1
        trev = None
        for c in range(4):
            w = [e5, ev_idb]
            if c == 0:
                w.append(ps_free.get(pid))
            trev = p.tr(psap(slotA, c * 128, [128, [1, 128]]),
                        AP(evb, 1024 + c * 128, [[2048, 128], [1, 128]]),
                        ident_ap, inc=(c == 3), waits=w,
                        start=(c == 0), stop=(c == 3))
        ehi = p.v_copy(sb(ev2, 0, 128, 0, [[1, 512]]),
                       psap(slotA, 0, [128, [1, 512]]),
                       waits=[trev, buf_free.get(("gather",))])
        elo = p.v_tt(ALU.subtract, sb(ev2, 0, 128, 512, [[1, 512]]),
                     psap(slotA, 0, [128, [1, 512]]),
                     sb(ev2, 0, 128, 0, [[1, 512]]))
        ps_free[pid] = elo
        buf_free[("ev",)] = elo
        d1 = p.dout(scat_hi, AP(ev2, 0, [[2048, 128], [128, 4], [1, 128]]),
                    waits=[elo])
        d2 = p.dout(scat_lo, AP(ev2, 512, [[2048, 128], [128, 4], [1, 128]]),
                    waits=[elo])
        buf_free[("gather",)] = d2
        return d2

    def mm3(slot, lhs_h, lhs_coff, lhs_lo_off, rhs_h, rhs_coff, rhs_lo_off,
            nk, kw, ow, first_waits, extra_mms=None):
        mm_ev = None
        nmm = nk * len(PASSES) + (len(extra_mms) if extra_mms else 0)
        idx = 0
        for c in range(nk):
            for (a, b) in PASSES:
                w = first_waits if idx == 0 else []
                last = (idx == nmm - 1)
                mm_ev = p.mm(psap(slot, 0, [128, [1, ow]]),
                             sb(lhs_h, 0, 128, a * lhs_lo_off + lhs_coff + c * kw, [[1, kw]]),
                             sb(rhs_h, 0, 128, b * rhs_lo_off + rhs_coff + c * ow, [[1, ow]]),
                             start=(idx == 0), stop=last, inc=last, waits=w)
                idx += 1
        return mm_ev

    ht_ev = ln_phase(x_d, ht_s, None)

    qt_write_evs = []
    kvT_ready = []
    slab_free = None
    last_mm = None
    buf_free[("slab",)] = None
    for sj in range(9):
        if sj == 0:
            spl = load_slab_split(wc_d, DL, 0, ND, DL, 0, [slab_free])
        else:
            spl = load_slab_split(wq_d, D, (sj - 1) * OS, ND, OS, 0, [slab_free])
        for i in range(NT):
            par = i % 2
            htev = p.din(sb(ht[par], 0, 128, 0, [[1, 2 * D]]),
                         AP(ht_s, i * 128 * 2 * D, [[2 * D, 128], [1, 2 * D]]),
                         [buf_free.get(("ht", par)), ht_ev[-1]])
            si = (sj * NT + i) % 4
            slot = PS[si]
            mm_ev = None
            idx = 0
            for c in range(ND):
                for (a, b) in PASSES:
                    w = ([htev] + spl + [ps_free.get(("PS", si))]) if idx == 0 else []
                    last = (idx == ND * 3 - 1)
                    mm_ev = p.mm(psap(slot, 0, [128, [1, OS]]),
                                 sb(ht[par], 0, 128, a * D + c * 128, [[1, 128]]),
                                 sb(slab, 0, 128, b * 16384 + c * OS, [[1, OS]]),
                                 start=(idx == 0), stop=last, inc=last, waits=w)
                    idx += 1
            last_mm = mm_ev
            buf_free[("ht", par)] = mm_ev
            if sj == 0:
                eg0 = p.v_copy(AP(evb, 1536, [[2048, 128], [1, 512]]),
                               psap(slot, 0, [128, [1, 512]]),
                               waits=[mm_ev, buf_free.get(("gather",))])
                ps_free[("PS", si)] = eg0
                pid = ("PA", pa_par[0])
                slotA = PA[pa_par[0]]
                pa_par[0] ^= 1
                trev = None
                for c in range(4):
                    w = [eg0, ev_idb]
                    if c == 0:
                        w.append(ps_free.get(pid))
                    trev = p.tr(psap(slotA, c * 128, [128, [1, 128]]),
                                AP(evb, 1536 + c * 128, [[2048, 128], [1, 128]]),
                                ident_ap, inc=(c == 3), waits=w,
                                start=(c == 0), stop=(c == 3))
                buf_free[("gather",)] = trev
                ekh = p.v_copy(AP(kvt, i * 128, [[16384, 128], [2048, 4], [1, 128]]),
                               AP(slotA[0], slotA[1],
                                  [[slotA[0].shape[1], 128], [128, 4], [1, 128]]),
                               waits=[trev])
                ekv = p.v_tt(ALU.subtract,
                             AP(kvt, 8192 + i * 128, [[16384, 128], [2048, 4], [1, 128]]),
                             AP(slotA[0], slotA[1],
                                [[slotA[0].shape[1], 128], [128, 4], [1, 128]]),
                             AP(kvt, i * 128, [[16384, 128], [2048, 4], [1, 128]]))
                ps_free[pid] = ekv
                kvT_ready.append(ekv)
            else:
                hb = 4 * (sj - 1)
                sc_hi = AP(qt_s, hb * 128 * 2 * S + i * 128,
                           [[2 * S, 128], [128 * 2 * S, 4], [1, 128]])
                sc_lo = AP(qt_s, hb * 128 * 2 * S + S + i * 128,
                           [[2 * S, 128], [128 * 2 * S, 4], [1, 128]])
                qt_write_evs.append(rope_evac(slot, ("PS", si), i, sc_hi, sc_lo, mm_ev))
        slab_free = last_mm

    kt_write_evs = []
    v_write_evs = []
    for sj in range(16):
        isk = sj < 8
        j = sj if isk else sj - 8
        wsrc_d = wk_d if isk else wv_d
        spl = load_slab_split(wsrc_d, D, j * OS, NL, OS, 0, [slab_free])
        for i in range(NT):
            si = i % 4
            slot = PS[si]
            mm_ev = None
            idx = 0
            for c in range(NL):
                for (a, b) in PASSES:
                    w = (spl + kvT_ready + [ps_free.get(("PS", si))]) if idx == 0 else []
                    last = (idx == NL * 3 - 1)
                    mm_ev = p.mm(psap(slot, 0, [128, [1, OS]]),
                                 AP(kvt, a * 8192 + c * 2048 + i * 128, [[16384, 128], [1, 128]]),
                                 sb(slab, 0, 128, b * 16384 + c * OS, [[1, OS]]),
                                 start=(idx == 0), stop=last, inc=last, waits=w)
                    idx += 1
            last_mm = mm_ev
            if isk:
                hb = 4 * j
                sc_hi = AP(kt_s, hb * 128 * 2 * S + i * 128,
                           [[2 * S, 128], [128 * 2 * S, 4], [1, 128]])
                sc_lo = AP(kt_s, hb * 128 * 2 * S + S + i * 128,
                           [[2 * S, 128], [128 * 2 * S, 4], [1, 128]])
                kt_write_evs.append(rope_evac(slot, ("PS", si), i, sc_hi, sc_lo, mm_ev))
            else:
                sh, sl = split(psap(slot, 0, [128, [1, 512]]),
                               sb(ev2, 0, 128, 0, [[1, 512]]),
                               sb(ev2, 0, 128, 512, [[1, 512]]),
                               waits=[mm_ev, buf_free.get(("ev2a",)),
                                      buf_free.get(("gather",))])
                ps_free[("PS", si)] = sl
                d1 = p.dout(AP(v_s, i * 128 * 2 * D + j * OS, [[2 * D, 128], [1, OS]]),
                            sb(ev2, 0, 128, 0, [[1, 512]]), waits=[sl])
                d2 = p.dout(AP(v_s, i * 128 * 2 * D + D + j * OS, [[2 * D, 128], [1, OS]]),
                            sb(ev2, 0, 128, 512, [[1, 512]]), waits=[sl])
                buf_free[("ev2a",)] = d2
                v_write_evs.append(d2)
        slab_free = last_mm

    phaseA_end = [qt_write_evs[-1], kt_write_evs[-1], v_write_evs[-1]]
    a3_t = p.now("tensor", "st")
    a3_v = p.now("vector", "sv")
    buf_free[("bq", 0)] = buf_free[("bq", 1)] = a3_t
    ps_free[("psT", 0)] = ps_free[("psT", 1)] = a3_t
    buf_free[("zrep",)] = buf_free[("atev",)] = a3_v
    buf_free[("zrow",)] = buf_free[("psTf",)] = a3_v

    at_write_evs = []
    qkv_ev = {}
    for h in range(NH + 1):
        if h < NH:
            ph = h % 2
            base = ph * 16384
            w = [buf_free.get(("bq", ph))] + (phaseA_end if h < 2 else [])
            e_q = p.din(AP(slab, base, [[32768, 128], [1, 2 * S]]),
                        AP(qt_s, h * 128 * 2 * S, [[2 * S, 128], [1, 2 * S]]), w)
            e_k = p.din(AP(slab, base + 4096, [[32768, 128], [1, 2 * S]]),
                        AP(kt_s, h * 128 * 2 * S, [[2 * S, 128], [1, 2 * S]]), w)
            e_v = []
            for comp in range(2):
                for k in range(2):
                    e_v.append(p.din(
                        AP(slab, base + 8192 + comp * 2048 + k * 1024,
                           [[32768, 128], [128, 8], [1, 128]]),
                        AP(v_s, comp * D + h * 128 + k * 8 * 128 * 2 * D,
                           [[2 * D, 128], [128 * 2 * D, 8], [1, 128]]), w))
            qkv_ev[h] = [e_q, e_k] + e_v
        if h == 0:
            continue
        hh = h - 1
        ph = hh % 2
        base = ph * 16384
        for g in range(2):
            last_z = None
            for j in range(16):
                pj = j % 2
                pbS = pbS0 if pj == 0 else pbS1
                psTb = pj * 2048
                wqk = qkv_ev[hh] + [ps_free.get(("BS", pj))]
                idx = 0
                for half in range(2):
                    for (a, b) in PASSES:
                        w = wqk if idx == 0 else []
                        mm_ev = p.mm(AP(pbS, half * 512, [[1024, 128], [1, 512]]),
                                     AP(slab, base + 4096 + a * 2048 + j * 128,
                                        [[32768, 128], [1, 128]]),
                                     AP(slab, base + b * 2048 + g * 1024 + half * 512,
                                        [[32768, 128], [1, 512]]),
                                     start=(idx % 3 == 0), stop=(idx % 3 == 2),
                                     inc=(idx == 5), waits=w)
                        idx += 1
                e_qk = p.now("tensor", "st")
                e_ex = p.s_act(AP(evb, 0, [[2048, 128], [1, 1024]]),
                               AP(pbS, 0, [[1024, 128], [1, 1024]]),
                               AF.Exp, scale=cQKS,
                               waits=[e_qk, ev_cst, buf_free.get(("psTf",))])
                ps_free[("BS", pj)] = e_ex
                sh, sl = split(AP(evb, 0, [[2048, 128], [1, 1024]]),
                               AP(kvt, psTb, [[16384, 128], [1, 1024]]),
                               AP(kvt, psTb + 1024, [[16384, 128], [1, 1024]]),
                               waits=[e_ex, ps_free.get(("psT", pj))])
                buf_free[("psTf",)] = sl
                idx = 0
                for half in range(2):
                    for (a, b) in PASSES:
                        w = [sl]
                        if j == 0 and idx == 0:
                            w += [ps_free.get(("pbA",))]
                        p.mm(AP(pbA, half * 512, [[1024, 128], [1, 512]]),
                             AP(slab, base + 8192 + a * 2048 + j * 128,
                                [[32768, 128], [1, 128]]),
                             AP(kvt, psTb + b * 1024 + half * 512,
                                [[16384, 128], [1, 512]]),
                             start=(j == 0 and idx % 3 == 0),
                             stop=(j == 15 and idx % 3 == 2),
                             inc=False, waits=w)
                        idx += 1
                for half in range(2):
                    for comp in range(2):
                        w = []
                        if j == 0 and half == 0 and comp == 0:
                            w = [ev_onb, ps_free.get(("pbZ",))]
                        last_z = p.mm(AP(pbZ, half * 512, [[1024, 1], [1, 512]]),
                                      onesb_ap,
                                      AP(kvt, psTb + comp * 1024 + half * 512,
                                         [[16384, 128], [1, 512]]),
                                      start=(j == 0 and comp == 0),
                                      stop=(j == 15 and comp == 1),
                                      inc=(half == 1 and comp == 1), waits=w)
                ps_free[("psT", pj)] = last_z
            e_zc = p.v_copy(AP(xa[0], 1024, [[4096, 1], [1, 1024]]),
                            AP(pbZ, 0, [[1024, 1], [1, 1024]]),
                            waits=[last_z, buf_free.get(("zrow",))])
            ps_free[("pbZ",)] = e_zc
            e_zr = p.v_recip(AP(xa[0], 2048, [[4096, 1], [1, 1024]]),
                             AP(xa[0], 1024, [[4096, 1], [1, 1024]]))
            bc_ev = None
            for half in range(2):
                w = ([e_zr, ev_onr, ps_free.get(("BS", 0))] if half == 0 else [])
                bc_ev = p.mm(AP(pbS0, half * 512, [[1024, 128], [1, 512]]),
                             onesr_ap,
                             AP(xa[0], 2048 + half * 512, [[4096, 1], [1, 512]]),
                             start=True, stop=True, inc=(half == 1), waits=w)
            e_zv = p.v_copy(AP(evb, 1024, [[2048, 128], [1, 1024]]),
                            AP(pbS0, 0, [[1024, 128], [1, 1024]]),
                            waits=[bc_ev, buf_free.get(("zrep",))])
            ps_free[("BS", 0)] = e_zv
            e_at = p.v_tt(ALU.mult, AP(xa[0], 0, [[4096, 128], [1, 1024]]),
                          AP(pbA, 0, [[1024, 128], [1, 1024]]),
                          AP(evb, 1024, [[2048, 128], [1, 1024]]),
                          waits=[e_zv, buf_free.get(("atev",)), last_z])
            ps_free[("pbA",)] = e_at
            buf_free[("zrep",)] = e_at
            buf_free[("zrow",)] = e_zr
            sh, sl = split(AP(xa[0], 0, [[4096, 128], [1, 1024]]),
                           sb(ev2, 0, 128, 0, [[1, 1024]]),
                           sb(ev2, 0, 128, 1024, [[1, 1024]]),
                           waits=[e_at, buf_free.get(("ev2a",))])
            buf_free[("atev",)] = sl
            d1 = p.dout(AP(at_s, hh * 128 * 2 * S + g * 1024, [[2 * S, 128], [1, 1024]]),
                        sb(ev2, 0, 128, 0, [[1, 1024]]), waits=[sl])
            d2 = p.dout(AP(at_s, hh * 128 * 2 * S + S + g * 1024, [[2 * S, 128], [1, 1024]]),
                        sb(ev2, 0, 128, 1024, [[1, 1024]]), waits=[sl])
            buf_free[("ev2a",)] = d2
            at_write_evs.append(d2)
        buf_free[("bq", ph)] = p.now("tensor", "st")

    b_end_v = p.now("vector", "sv")
    b_end_out = p.now("gpsimd", "sout")
    buf_free[("xsl", 0)] = buf_free[("xsl", 1)] = b_end_v
    buf_free[("eo",)] = b_end_v
    buf_free[("ev",)] = b_end_v
    buf_free[("gather",)] = b_end_v
    buf_free[("xa", 0)] = buf_free[("xa", 1)] = b_end_v
    buf_free[("slab",)] = p.now("tensor", "st")

    o1_tile_ev = {}
    slab_free = p.now("tensor", "st")
    for sj in range(8):
        spl = load_slab_split(wo_d, D, sj * OS, ND, OS, 0, [slab_free])
        for i in range(NT):
            par = i % 2
            w = [buf_free.get(("ht", par))] + (at_write_evs if (sj == 0 and i == 0) else [])
            a_evs = []
            for comp in range(2):
                for k in range(4):
                    a_evs.append(p.din(
                        sb(ht[par], 0, 128, comp * D + k * 1024, [[128, 8], [1, 128]]),
                        AP(at_s, comp * S + (8 * k) * 128 * 2 * S + i * 128,
                           [[2 * S, 128], [128 * 2 * S, 8], [1, 128]]), w))
            exs = p.din(AP(evb, par * 512, [[2048, 128], [1, 512]]),
                        AP(x_d, i * 128 * D + sj * OS, [[D, 128], [1, OS]]),
                        [buf_free.get(("xsl", par))])
            si = i % 4
            slot = PS[si]
            mm_ev = None
            idx = 0
            for c in range(ND):
                for (a, b) in PASSES:
                    w = (a_evs + spl + [ps_free.get(("PS", si))]) if idx == 0 else []
                    last = (idx == ND * 3 - 1)
                    mm_ev = p.mm(psap(slot, 0, [128, [1, OS]]),
                                 sb(ht[par], 0, 128, a * D + c * 128, [[1, 128]]),
                                 sb(slab, 0, 128, b * 16384 + c * OS, [[1, OS]]),
                                 start=(idx == 0), stop=last, inc=last, waits=w)
                    idx += 1
            last_mm = mm_ev
            buf_free[("ht", par)] = mm_ev
            e_o1 = p.v_tt(ALU.add, AP(evb, 1024, [[2048, 128], [1, 512]]),
                          psap(slot, 0, [128, [1, 512]]),
                          AP(evb, par * 512, [[2048, 128], [1, 512]]),
                          waits=[mm_ev, exs, buf_free.get(("eo",))])
            ps_free[("PS", si)] = e_o1
            buf_free[("xsl", par)] = e_o1
            dev = p.dout(AP(o1_s, i * 128 * D + sj * OS, [[D, 128], [1, OS]]),
                         AP(evb, 1024, [[2048, 128], [1, 512]]), waits=[e_o1])
            buf_free[("eo",)] = dev
            o1_tile_ev[(sj, i)] = dev
        slab_free = last_mm

    c1_last = o1_tile_ev[(7, NT - 1)]

    lt_ev = ln_phase(o1_s, lt_s, c1_last)

    spl_a = load_slab_split(a2t_d, 128, 0, ND, 128, 0, [slab_free])
    spl_r = load_slab_split(rwt_d, 8, 0, ND, 8, 4096, [slab_free])
    t_evs = []
    rl_evs = []
    mm_r = None
    for i in range(NT):
        par = i % 2
        htev = p.din(sb(ht[par], 0, 128, 0, [[1, 2 * D]]),
                     AP(lt_s, i * 128 * 2 * D, [[2 * D, 128], [1, 2 * D]]),
                     [buf_free.get(("ht", par)), lt_ev[-1]])
        mm_t = None
        idx = 0
        for c in range(ND):
            for (a, b) in PASSES:
                w = ([htev] + spl_a + spl_r + [ps_free.get(("PS", 0))]) if idx == 0 else []
                w2 = [ps_free.get(("PS", 2))] if idx == 0 else []
                last = (idx == ND * 3 - 1)
                mm_t = p.mm(AP(pbS0, 0, [[1024, 128], [1, 128]]),
                            sb(ht[par], 0, 128, a * D + c * 128, [[1, 128]]),
                            sb(slab, 0, 128, b * 16384 + c * 128, [[1, 128]]),
                            start=(idx == 0), stop=last, inc=last, waits=w)
                mm_r = p.mm(AP(pbS1, 0, [[1024, 128], [1, 8]]),
                            sb(ht[par], 0, 128, a * D + c * 128, [[1, 128]]),
                            sb(slab, 0, 128, b * 16384 + 4096 + c * 8, [[1, 8]]),
                            start=(idx == 0), stop=last, inc=last, waits=w2)
                idx += 1
        buf_free[("ht", par)] = mm_r
        e_t = p.v_copy(AP(hbuf, i * 128, [[4096, 128], [1, 128]]),
                       AP(pbS0, 0, [[1024, 128], [1, 128]]), waits=[mm_t])
        ps_free[("PS", 0)] = e_t
        e_rl = p.v_copy(AP(evb, i * 8, [[2048, 128], [1, 8]]),
                        AP(pbS1, 0, [[1024, 128], [1, 8]]), waits=[mm_r])
        ps_free[("PS", 2)] = e_rl
        t_evs.append(e_t)
        rl_evs.append(e_rl)
    lg_mm = None
    for i in range(NT):
        w = [rl_evs[i], ev_msk] + ([ps_free.get(("pbZ",))] if i == 0 else [])
        lg_mm = p.mm(AP(pbZ, 0, [[1024, 1], [1, 8]]),
                     sb(maskc, 0, 128, i, [[1, 1]]),
                     AP(evb, i * 8, [[2048, 128], [1, 8]]),
                     start=(i == 0), stop=(i == NT - 1),
                     inc=(i == NT - 1), waits=w)
    lg = AP(smt, 0, [[64, 1], [1, 8]])
    mx = AP(smt, 8, [[64, 1], [1, 1]])
    nmx = AP(smt, 9, [[64, 1], [1, 1]])
    pr = AP(smt, 16, [[64, 1], [1, 8]])
    sm1 = AP(smt, 24, [[64, 1], [1, 1]])
    rsm = AP(smt, 25, [[64, 1], [1, 1]])
    rout = AP(smt, 32, [[64, 1], [1, 8]])
    e1 = p.v_tt(ALU.add, lg, AP(pbZ, 0, [[1024, 1], [1, 8]]),
                AP(routerb, 0, [[8, 1], [1, 8]]), waits=[lg_mm, ev_rb])
    ps_free[("pbZ",)] = e1
    p.v_red(ALU.max, mx, lg)
    p.v_ts(nmx, mx, -1.0, None, ALU.mult, ALU.bypass)
    e2 = p.s_act(pr, lg, AF.Exp, bias=nmx, scale=1.0, accum=sm1,
                 waits=[p.now("vector", "sv")])
    e3r = p.v_recip(rsm, sm1, waits=[e2])
    e4 = p.v_ts(rout, pr, rsm, 2.0, ALU.mult, ALU.mult, waits=[e2],
                fwaits=[e3r])
    e5 = p.mm(AP(pbS0, 0, [[1024, 128], [1, 8]]), onesr_ap, rout,
              start=True, stop=True, inc=True,
              waits=[e4, ev_onr, ps_free.get(("PS", 0))])
    e6 = p.v_copy(AP(st2, 0, [[8, 128], [1, 8]]),
                  AP(pbS0, 0, [[1024, 128], [1, 8]]), waits=[e5])
    ps_free[("PS", 0)] = e6
    e7 = None
    for ee in range(8):
        e7 = p.v_ts(AP(xa[1], ee * 16, [[4096, 128], [128, 16], [1, 16]]),
                    AP(hbuf, ee * 16, [[4096, 128], [128, 16], [1, 16]]),
                    sb(st2, 0, 128, ee, [[1, 1]]), None, ALU.mult, ALU.bypass,
                    waits=t_evs + [buf_free.get(("xa", 1))] if ee == 0 else [],
                    fwaits=[e6] if ee == 0 else [])
    ts_evs = []
    for i in range(NT):
        pid = ("PA", pa_par[0])
        slotA = PA[pa_par[0]]
        pa_par[0] ^= 1
        trev = p.tr(psap(slotA, 0, [128, [1, 128]]),
                    AP(xa[1], i * 128, [[4096, 128], [1, 128]]),
                    ident_ap, inc=True,
                    waits=[e7, ev_idb, ps_free.get(pid)],
                    start=True, stop=True)
        e8h = p.v_copy(AP(kvt, i * 128, [[16384, 128], [1, 128]]),
                       psap(slotA, 0, [128, [1, 128]]), waits=[trev])
        e8 = p.v_tt(ALU.subtract,
                    AP(kvt, 2048 + i * 128, [[16384, 128], [1, 128]]),
                    psap(slotA, 0, [128, [1, 128]]),
                    AP(kvt, i * 128, [[16384, 128], [1, 128]]))
        ps_free[pid] = e8
        ts_evs.append(e8)
    slab_free = mm_r
    c2b_t = p.now("tensor", "st")
    buf_free[("brep",)] = buf_free[("tmp",)] = buf_free[("eo2",)] = c2b_t
    buf_free[("xo1", 0)] = buf_free[("xo1", 1)] = c2b_t

    exv = p.din(sb(xa[0], 0, 128, 0, [[1, D]]), AP(b2_d, 0, [[D, 128], [1, D]]),
                ts_evs + [b_end_out, buf_free.get(("xa", 0))])
    eb2h = p.v_copy(AP(kvt, 4096, [[16384, 128], [1, D]]),
                    sb(xa[0], 0, 128, 0, [[1, D]]), waits=[exv])
    eb2l = p.v_tt(ALU.subtract, AP(kvt, 8192, [[16384, 128], [1, D]]),
                  sb(xa[0], 0, 128, 0, [[1, D]]),
                  AP(kvt, 4096, [[16384, 128], [1, D]]))
    buf_free[("xa", 0)] = eb2l
    out_evs = []
    e_t1 = None
    for sj in range(8):
        spl = load_slab_split(wf_d, D, sj * OS, ND, OS, 0, [slab_free])
        e_br = p.din(AP(hbuf, 2048, [[D, 128], [1, 512]]),
                     AP(brep_d, sj * OS, [[D, 128], [1, OS]]),
                     [buf_free.get(("brep",))])
        for i in range(NT):
            par = i % 2
            htev = p.din(sb(ht[par], 0, 128, 0, [[1, 2 * D]]),
                         AP(lt_s, i * 128 * 2 * D, [[2 * D, 128], [1, 2 * D]]),
                         [buf_free.get(("ht", par)), lt_ev[-1]])
            exs = p.din(AP(evb, par * 512, [[2048, 128], [1, 512]]),
                        AP(o1_s, i * 128 * D + sj * OS, [[D, 128], [1, OS]]),
                        [buf_free.get(("xo1", par)), o1_tile_ev[(sj, i)]])
            si = i % 4
            slot = PS[si]
            mm_ev = None
            idx = 0
            for c in range(ND):
                for (a, b) in PASSES:
                    w = ([htev] + spl + [ps_free.get(("PS", si))]) if idx == 0 else []
                    mm_ev = p.mm(psap(slot, 0, [128, [1, OS]]),
                                 sb(ht[par], 0, 128, a * D + c * 128, [[1, 128]]),
                                 sb(slab, 0, 128, b * 16384 + c * OS, [[1, OS]]),
                                 start=(idx == 0), stop=False, inc=False, waits=w)
                    idx += 1
            for pi, (a, b) in enumerate(PASSES):
                mm_ev = p.mm(psap(slot, 0, [128, [1, OS]]),
                             AP(kvt, a * 2048 + i * 128, [[16384, 128], [1, 128]]),
                             AP(kvt, 4096 + b * 4096 + sj * OS, [[16384, 128], [1, OS]]),
                             start=False, stop=(pi == 2), inc=(pi == 2),
                             waits=[eb2l] if pi == 0 else [])
            last_mm = mm_ev
            buf_free[("ht", par)] = mm_ev
            e_t1 = p.v_tt(ALU.add, AP(hbuf, 2560, [[D, 128], [1, 512]]),
                          psap(slot, 0, [128, [1, 512]]),
                          AP(hbuf, 2048, [[D, 128], [1, 512]]),
                          waits=[mm_ev, e_br, buf_free.get(("tmp",))])
            ps_free[("PS", si)] = e_t1
            e_t2 = p.v_tt(ALU.add, AP(hbuf, 3072, [[D, 128], [1, 512]]),
                          AP(hbuf, 2560, [[D, 128], [1, 512]]),
                          AP(evb, par * 512, [[2048, 128], [1, 512]]),
                          waits=[exs, buf_free.get(("eo2",))])
            buf_free[("tmp",)] = e_t2
            buf_free[("xo1", par)] = e_t2
            dev = p.dout(AP(out_d, i * 128 * D + sj * OS, [[D, 128], [1, OS]]),
                         AP(hbuf, 3072, [[D, 128], [1, 512]]), waits=[e_t2])
            buf_free[("eo2",)] = dev
            out_evs.append(dev)
        slab_free = last_mm
        buf_free[("brep",)] = e_t1

    p.ops["gpsimd"].append(("w", "sout", p.cnt["sout"]))

    with (
        nc.semaphore("sin") as h_sin,
        nc.semaphore("sout") as h_sout,
        nc.semaphore("st") as h_st,
        nc.semaphore("sv") as h_sv,
        nc.semaphore("ss") as h_ss,
        nc.Block() as block,
    ):
        sems = {"sin": h_sin, "sout": h_sout, "st": h_st, "sv": h_sv, "ss": h_ss}

        def mkbody(eng_ops):
            def body(eng):
                for op in eng_ops:
                    if op[0] == "w":
                        eng.wait_ge(sems[op[1]], op[2])
                    else:
                        inst = op[1](eng)
                        for (s, k) in op[2]:
                            inst.then_inc(sems[s], k)
            return body

        block.tensor(mkbody(p.ops["tensor"]))
        block.vector(mkbody(p.ops["vector"]))
        block.scalar(mkbody(p.ops["scalar"]))
        block.sync(mkbody(p.ops["sync"]))
        block.gpsimd(mkbody(p.ops["gpsimd"]))

    return nc


def _host_inputs(inputs):
    f32 = np.float32
    Wq = np.ascontiguousarray(inputs["L0_attn_Wq"], f32)
    Wc = np.ascontiguousarray(inputs["L0_attn_Wc"], f32)
    Wk = np.ascontiguousarray(inputs["L0_attn_Wk"], f32)
    Wv = np.ascontiguousarray(inputs["L0_attn_Wv"], f32)
    Wo = np.ascontiguousarray(inputs["L0_attn_Wo"], f32)
    Wf = np.ascontiguousarray(np.asarray(inputs["L0_ffn_W"]).T, f32)
    A = np.asarray(inputs["L0_ffn_A"], f32)
    B = np.asarray(inputs["L0_ffn_B"], f32)
    A2T = np.ascontiguousarray(A.reshape(128, D).T, f32)
    B2 = np.ascontiguousarray(np.transpose(B, (0, 2, 1)).reshape(128, D), f32)
    RWT = np.ascontiguousarray(np.asarray(inputs["L0_ffn_router_W"]).T, f32)
    brep = np.ascontiguousarray(
        np.broadcast_to(np.asarray(inputs["L0_ffn_b"], f32), (128, D)), f32)
    routerb = np.ascontiguousarray(
        np.asarray(inputs["L0_ffn_router_b"], f32).reshape(1, 8), f32)

    pos = np.arange(S, dtype=np.float64)
    inv = 1.0 / (10000.0 ** (np.arange(0, 128, 2, dtype=np.float64) / 128.0))
    ang = pos[:, None] * inv[None, :]
    ang = np.concatenate([ang, ang], axis=1)
    cos = np.cos(ang).astype(f32)
    sin = np.sin(ang).astype(f32)
    costok = np.zeros((128, S), f32)
    sintok = np.zeros((128, S), f32)
    for i in range(NT):
        costok[:, i * 128:(i + 1) * 128] = cos[i * 128:(i + 1) * 128, :]
        sintok[:, i * 128:(i + 1) * 128] = sin[i * 128:(i + 1) * 128, :]

    maskc = np.zeros((128, NT), f32)
    for t in range(611, 2047):
        maskc[t % 128, t // 128] = np.float32(np.float32(1.0) / np.float32(1436.0))

    import ml_dtypes
    identf = np.eye(128, dtype=f32)
    onesb = np.ones((128, 1), ml_dtypes.bfloat16)
    onesr = np.ones((1, 128), f32)
    consts = np.zeros((128, 4), f32)
    consts[:, 0] = NEPS
    consts[:, 1] = -0.5
    consts[:, 2] = QK_SCALE

    return dict(wq=Wq, wc=Wc, wk=Wk, wv=Wv, wo=Wo, wf=Wf, a2t=A2T, b2=B2,
                rwt=RWT, brep=brep, routerb=routerb, costok=costok,
                sintok=sintok, maskc=maskc, identf=identf, onesb=onesb,
                onesr=onesr, consts=consts)


def kernel(**inputs):
    xfull = np.ascontiguousarray(inputs["L0_attn_hidden_states"], np.float32)
    com = _host_inputs(inputs)
    nc = build_nc()
    in_maps = [dict(com, x=xfull[b]) for b in range(8)]
    res = run_bass_kernel_spmd(nc, in_maps, core_ids=list(range(8)))
    return np.stack([res.results[b]["out"] for b in range(8)], axis=0)


# version 06
# speedup vs baseline: 1.6792x; 1.4113x over previous
import numpy as np
import concourse.bass as bass
import concourse.mybir as mybir
from concourse.bass_utils import run_bass_kernel_spmd

F32 = mybir.dt.float32
BF16 = mybir.dt.bfloat16
AF = mybir.ActivationFunctionType
ALU = mybir.AluOpType
AX = mybir.AxisListType

S, D, DL = 2048, 4096, 512
NT, ND, NL, NH = 16, 32, 4, 32
OS = 512
NEPS = 1e-5
QK_SCALE = float(1.0 / np.sqrt(128.0))
PASSES = ((0, 0), (0, 1), (1, 0))

ENGINES = ("tensor", "vector", "scalar", "sync", "gpsimd")
DMA_SEMS = ("sin", "sout")


def AP(h, off, dims):
    return bass.AP(h, off, [list(d) for d in dims])


class Plan:
    def __init__(self):
        self.ops = {e: [] for e in ENGINES}
        self.cnt = {"sin": 0, "sout": 0, "st": 0, "sv": 0, "ss": 0}

    def now(self, eng, sem):
        return (eng, sem, self.cnt[sem])

    def emit(self, eng, fn, incs=(), waits=(), fwaits=()):
        lst = self.ops[eng]
        merged = {}
        for w in waits:
            if w is None:
                continue
            weng, wsem, wval = w
            if weng == eng and wsem not in DMA_SEMS:
                continue
            if wval <= 0:
                continue
            merged[wsem] = max(merged.get(wsem, 0), wval)
        for w in fwaits:
            if w is None:
                continue
            weng, wsem, wval = w
            if wval <= 0:
                continue
            merged[wsem] = max(merged.get(wsem, 0), wval)
        for wsem, wval in merged.items():
            lst.append(("w", wsem, wval))
        lst.append(("i", fn, tuple(incs)))
        ev = None
        for (s, k) in incs:
            self.cnt[s] += k
            ev = (eng, s, self.cnt[s])
        return ev

    def din(self, dst, src, waits=(), fence=True):
        ev = self.emit("sync",
                       lambda e, d=dst, s=src: e.dma_start(out=d, in_=s),
                       [("sin", 16)], waits)
        if fence:
            self.ops["sync"].append(("w", "sin", ev[2]))
        return ev

    def dout(self, dst, src, waits=()):
        return self.emit("gpsimd",
                         lambda e, d=dst, s=src: e.dma_start(out=d, in_=s),
                         [("sout", 16)], waits)

    def mm(self, out, lhsT, rhs, start, stop, inc=False, waits=()):
        return self.emit(
            "tensor",
            lambda e, o=out, l=lhsT, r=rhs, a=start, b=stop:
                e.matmul(o, l, r, start=a, stop=b),
            [("st", 1)] if inc else [], waits)

    def tr(self, out, in_, ident, inc=False, waits=(), start=True, stop=True):
        return self.emit(
            "tensor",
            lambda e, o=out, i=in_, d=ident, a=start, b=stop:
                e.matmul(o, i, d, start=a, stop=b, is_transpose=True),
            [("st", 1)] if inc else [], waits)

    def v_tt(self, op, out, a, b, waits=()):
        return self.emit(
            "vector",
            lambda e, o=out, x=a, y=b, p_=op: e.tensor_tensor(o, x, y, p_),
            [("sv", 1)], waits)

    def v_ts(self, out, in0, s1, s2, op0, op1, waits=(), fwaits=()):
        if s2 is None:
            return self.emit(
                "vector",
                lambda e, o=out, x=in0, a=s1, p0=op0:
                    e.tensor_single_scalar(o, x, a, p0),
                [("sv", 1)], waits, fwaits)
        return self.emit(
            "vector",
            lambda e, o=out, x=in0, a=s1, b=s2, p0=op0, p1=op1:
                e.tensor_scalar(o, x, a, b, p0, p1),
            [("sv", 1)], waits, fwaits)

    def v_red(self, op, out, in_, waits=()):
        return self.emit(
            "vector",
            lambda e, o=out, x=in_, p_=op: e.tensor_reduce(o, x, AX.X, p_),
            [("sv", 1)], waits)

    def v_copy(self, out, in_, waits=()):
        return self.emit(
            "vector",
            lambda e, o=out, x=in_: e.tensor_copy(o, x),
            [("sv", 1)], waits)

    def v_recip(self, out, in_, waits=()):
        return self.emit(
            "vector",
            lambda e, o=out, x=in_: e.reciprocal(o, x),
            [("sv", 1)], waits)

    def s_act(self, out, in_, func, bias=0.0, scale=1.0, accum=None, waits=()):
        return self.emit(
            "scalar",
            lambda e, o=out, x=in_, f=func, b=bias, s=scale, ac=accum:
                e.activation(o, x, f, bias=b, scale=s, accum_out=ac),
            [("ss", 1)], waits)


def build_nc():
    nc = bass.Bass("TRN2", target_bir_lowering=False)
    p = Plan()

    x_d = nc.dram_tensor("x", [S, D], F32, kind="ExternalInput")
    wq_d = nc.dram_tensor("wq", [D, D], F32, kind="ExternalInput")
    wc_d = nc.dram_tensor("wc", [D, DL], F32, kind="ExternalInput")
    wk_d = nc.dram_tensor("wk", [DL, D], F32, kind="ExternalInput")
    wv_d = nc.dram_tensor("wv", [DL, D], F32, kind="ExternalInput")
    wo_d = nc.dram_tensor("wo", [D, D], F32, kind="ExternalInput")
    wf_d = nc.dram_tensor("wf", [D, D], F32, kind="ExternalInput")
    a2t_d = nc.dram_tensor("a2t", [D, 128], F32, kind="ExternalInput")
    b2_d = nc.dram_tensor("b2", [128, D], F32, kind="ExternalInput")
    rwt_d = nc.dram_tensor("rwt", [D, 8], F32, kind="ExternalInput")
    brep_d = nc.dram_tensor("brep", [128, D], F32, kind="ExternalInput")
    costok_d = nc.dram_tensor("costok", [128, S], F32, kind="ExternalInput")
    sintok_d = nc.dram_tensor("sintok", [128, S], F32, kind="ExternalInput")
    maskc_d = nc.dram_tensor("maskc", [128, NT], F32, kind="ExternalInput")
    identf_d = nc.dram_tensor("identf", [128, 128], F32, kind="ExternalInput")
    onesb_d = nc.dram_tensor("onesb", [128, 1], BF16, kind="ExternalInput")
    onesr_d = nc.dram_tensor("onesr", [1, 128], F32, kind="ExternalInput")
    routerb_d = nc.dram_tensor("routerb", [1, 8], F32, kind="ExternalInput")
    consts_d = nc.dram_tensor("consts", [128, 4], F32, kind="ExternalInput")
    out_d = nc.dram_tensor("out", [S, D], F32, kind="ExternalOutput")

    SK = globals().get("SCRATCH_KIND", "Internal")
    ht_s = nc.dram_tensor("ht_s", [NT, 128, 2 * D], BF16, kind=SK)
    qt_s = nc.dram_tensor("qt_s", [NH, 128, 2 * S], BF16, kind=SK)
    kt_s = nc.dram_tensor("kt_s", [NH, 128, 2 * S], BF16, kind=SK)
    v_s = nc.dram_tensor("v_s", [NT, 128, 2 * D], BF16, kind=SK)
    at_s = nc.dram_tensor("at_s", [NH, 128, 2 * S], BF16, kind=SK)
    o1_s = nc.dram_tensor("o1_s", [S, D], F32, kind=SK)
    lt_s = nc.dram_tensor("lt_s", [NT, 128, 2 * D], BF16, kind=SK)

    costok = nc.alloc_sbuf_tensor("costok_sb", [128, S], F32)
    sintok = nc.alloc_sbuf_tensor("sintok_sb", [128, S], F32)
    maskc = nc.alloc_sbuf_tensor("maskc_sb", [128, NT], F32)
    identf = nc.alloc_sbuf_tensor("identf_sb", [128, 128], F32)
    onesb = nc.alloc_sbuf_tensor("onesb_sb", [128, 1], BF16)
    onesr = nc.alloc_sbuf_tensor("onesr_sb", [1, 128], F32)
    routerb = nc.alloc_sbuf_tensor("routerb_sb", [1, 8], F32)
    consts = nc.alloc_sbuf_tensor("consts_sb", [128, 4], F32)
    xa = [nc.alloc_sbuf_tensor(f"xa{i}", [128, D], F32) for i in range(2)]
    hbuf = nc.alloc_sbuf_tensor("hbuf", [128, D], F32)
    ht = [nc.alloc_sbuf_tensor(f"ht{i}", [128, 2 * D], BF16) for i in range(2)]
    slab = nc.alloc_sbuf_tensor("slab", [128, 32768], BF16)
    kvt = nc.alloc_sbuf_tensor("kvt", [128, 16384], BF16)
    evb = nc.alloc_sbuf_tensor("evb", [128, 2048], F32)
    ev2 = nc.alloc_sbuf_tensor("ev2", [128, 2048], BF16)
    st1 = nc.alloc_sbuf_tensor("st1", [128, 16], F32)
    st2 = nc.alloc_sbuf_tensor("st2", [128, 8], F32)
    smt = nc.alloc_sbuf_tensor("smt", [1, 64], F32)

    pbS0 = nc.alloc_psum_tensor("pbS0", [128, 1024], F32)
    pbS1 = nc.alloc_psum_tensor("pbS1", [128, 1024], F32)
    pbA = nc.alloc_psum_tensor("pbA", [128, 1024], F32)
    pbZ = nc.alloc_psum_tensor("pbZ", [1, 1024], F32)
    PS = [(pbS0, 0), (pbS0, 512), (pbS1, 0), (pbS1, 512)]
    PA = [(pbA, 0), (pbA, 512)]

    ps_free = {}
    buf_free = {}

    def sb(h, poff, plen, coff, dims):
        fl = h.shape[1]
        return AP(h, poff * fl + coff, [[fl, plen]] + [list(d) for d in dims])

    def psap(slot, coff, dims):
        t, base = slot
        return AP(t, base + coff,
                  [[t.shape[1]] + [dims[0]]] + [list(d) for d in dims[1:]])

    ev_cos = p.din(sb(costok, 0, 128, 0, [[1, S]]), AP(costok_d, 0, [[S, 128], [1, S]]))
    ev_sin = p.din(sb(sintok, 0, 128, 0, [[1, S]]), AP(sintok_d, 0, [[S, 128], [1, S]]))
    ev_msk = p.din(sb(maskc, 0, 128, 0, [[1, NT]]), AP(maskc_d, 0, [[NT, 128], [1, NT]]))
    ev_idb = p.din(sb(identf, 0, 128, 0, [[1, 128]]), AP(identf_d, 0, [[128, 128], [1, 128]]))
    ev_onb = p.din(sb(onesb, 0, 128, 0, [[1, 1]]), AP(onesb_d, 0, [[1, 128], [1, 1]]))
    ev_onr = p.din(sb(onesr, 0, 1, 0, [[1, 128]]), AP(onesr_d, 0, [[128, 1], [1, 128]]))
    ev_rb = p.din(sb(routerb, 0, 1, 0, [[1, 8]]), AP(routerb_d, 0, [[8, 1], [1, 8]]))
    ev_cst = p.din(sb(consts, 0, 128, 0, [[1, 4]]), AP(consts_d, 0, [[4, 128], [1, 4]]))

    ident_ap = sb(identf, 0, 128, 0, [[1, 128]])
    onesb_ap = sb(onesb, 0, 128, 0, [[1, 1]])
    onesr_ap = sb(onesr, 0, 1, 0, [[1, 128]])
    cEPS = sb(consts, 0, 128, 0, [[1, 1]])
    cNHALF = sb(consts, 0, 128, 1, [[1, 1]])
    cQKS = sb(consts, 0, 128, 2, [[1, 1]])

    def split(src_f32, dst_hi, dst_lo, waits=()):
        e1 = p.v_copy(dst_hi, src_f32, waits=waits)
        e2 = p.v_tt(ALU.subtract, dst_lo, src_f32, dst_hi)
        return e1, e2

    def load_slab_split(w_dram, row_len, col_off, nchunks, ow, slab_coff,
                        waits):
        evs = []
        per_stage = max(1, 4096 // ow)
        for s0 in range(0, nchunks, per_stage):
            n = min(per_stage, nchunks - s0)
            par = (s0 // per_stage) % 2
            w = [buf_free.get(("xa", par))] + (list(waits) if s0 == 0 else [])
            src = AP(w_dram, col_off + s0 * 128 * row_len,
                     [[row_len, 128], [128 * row_len, n], [1, ow]])
            exv = p.din(sb(xa[par], 0, 128, 0, [[ow, n], [1, ow]]), src, w)
            ehi = p.v_copy(sb(slab, 0, 128, slab_coff + s0 * ow, [[1, n * ow]]),
                           sb(xa[par], 0, 128, 0, [[1, n * ow]]),
                           waits=[exv] + list(waits))
            elo = p.v_tt(ALU.subtract,
                         sb(slab, 0, 128, 16384 + slab_coff + s0 * ow, [[1, n * ow]]),
                         sb(xa[par], 0, 128, 0, [[1, n * ow]]),
                         sb(slab, 0, 128, slab_coff + s0 * ow, [[1, n * ow]]))
            buf_free[("xa", par)] = elo
            evs.append(elo)
        return evs

    def ln_phase(src_dram, dst_scratch, extra_wait):
        dst_ev = [None] * NT
        for i in range(NT):
            par = i % 2
            xap = sb(xa[par], 0, 128, 0, [[1, D]])
            xrd = p.din(xap, AP(src_dram, i * 128 * D, [[D, 128], [1, D]]),
                        [buf_free.get(("xa", par)), extra_wait])
            sums = sb(st1, 0, 128, 0, [[1, 1]])
            ssq = sb(st1, 0, 128, 1, [[1, 1]])
            hfull = sb(hbuf, 0, 128, 0, [[1, D]])
            p.v_red(ALU.add, sums, xap, [xrd])
            e2a = p.s_act(hfull, xap, AF.Square,
                          waits=[xrd, buf_free.get(("hbuf",))])
            e2 = p.v_red(ALU.add, ssq, hfull, [e2a])
            m = sb(st1, 0, 128, 2, [[1, 1]])
            ex2 = sb(st1, 0, 128, 3, [[1, 1]])
            var = sb(st1, 0, 128, 4, [[1, 1]])
            lnv = sb(st1, 0, 128, 5, [[1, 1]])
            rstd = sb(st1, 0, 128, 6, [[1, 1]])
            nmr = sb(st1, 0, 128, 7, [[1, 1]])
            m2 = sb(st1, 0, 128, 8, [[1, 1]])
            m_ev = p.v_ts(m, sums, 1.0 / D, None, ALU.mult, ALU.bypass)
            p.v_ts(ex2, ssq, 1.0 / D, None, ALU.mult, ALU.bypass)
            e_m2 = p.s_act(m2, m, AF.Square, waits=[m_ev])
            e3 = p.v_tt(ALU.subtract, var, ex2, m2, waits=[e_m2])
            e4 = p.s_act(lnv, var, AF.Ln, bias=cEPS, scale=1.0,
                         waits=[e3, ev_cst])
            e5 = p.s_act(rstd, lnv, AF.Exp, scale=cNHALF)
            e6 = p.v_ts(nmr, m, rstd, -1.0, ALU.mult, ALU.mult, waits=[e5])
            e7 = p.v_ts(hfull, xap, rstd, nmr,
                        ALU.mult, ALU.add, waits=[e2], fwaits=[e6])
            buf_free[("xa", par)] = e7
            evacs = []
            for g in range(8):
                si = g % 4
                slot = PS[si]
                trev = None
                for c in range(4):
                    w = [e7, ev_idb]
                    if c == 0:
                        w.append(ps_free.get(("PS", si)))
                    trev = p.tr(psap(slot, c * 128, [128, [1, 128]]),
                                sb(hbuf, 0, 128, g * 512 + c * 128, [[1, 128]]),
                                ident_ap, inc=(c == 3), waits=w,
                                start=(c == 0), stop=(c == 3))
                w = [trev]
                if g == 0:
                    w.append(buf_free.get(("ht", par)))
                ehi = p.v_copy(sb(ht[par], 0, 128, g * 512, [[1, 512]]),
                               psap(slot, 0, [128, [1, 512]]), waits=w)
                elo = p.v_tt(ALU.subtract,
                             sb(ht[par], 0, 128, D + g * 512, [[1, 512]]),
                             psap(slot, 0, [128, [1, 512]]),
                             sb(ht[par], 0, 128, g * 512, [[1, 512]]))
                ps_free[("PS", si)] = elo
                evacs.append(elo)
            buf_free[("hbuf",)] = evacs[-1]
            dst_ev[i] = p.dout(AP(dst_scratch, i * 128 * 2 * D, [[2 * D, 128], [1, 2 * D]]),
                               sb(ht[par], 0, 128, 0, [[1, 2 * D]]), waits=evacs)
            buf_free[("ht", par)] = dst_ev[i]
        return dst_ev

    pa_par = [0]

    def rope_evac(slot, slot_id, i, scat_hi, scat_lo, mm_ev):
        t, base = slot
        fl = t.shape[1]
        ps3 = lambda off, n: AP(t, base + off, [[fl, 128], [128, 4], [1, n]])
        cos3 = lambda off, n: AP(costok, i * 128 + off, [[S, 128], [0, 4], [1, n]])
        sin3 = lambda off, n: AP(sintok, i * 128 + off, [[S, 128], [0, 4], [1, n]])
        tmpc = lambda off, n: AP(evb, off, [[2048, 128], [128, 4], [1, n]])
        tmps = lambda off, n: AP(evb, 512 + off, [[2048, 128], [128, 4], [1, n]])
        eo3 = lambda off, n: AP(evb, 1024 + off, [[2048, 128], [128, 4], [1, n]])
        w0 = [mm_ev, ev_cos, ev_sin, buf_free.get(("ev",))]
        p.v_tt(ALU.mult, tmpc(0, 128), ps3(0, 128), cos3(0, 128), waits=w0)
        p.v_tt(ALU.mult, tmps(0, 64), ps3(64, 64), sin3(0, 64))
        e3 = p.v_tt(ALU.mult, tmps(64, 64), ps3(0, 64), sin3(64, 64))
        ps_free[slot_id] = e3
        p.v_tt(ALU.subtract, eo3(0, 64), tmpc(0, 64), tmps(0, 64))
        e5 = p.v_tt(ALU.add, eo3(64, 64), tmpc(64, 64), tmps(64, 64))
        pid = ("PA", pa_par[0])
        slotA = PA[pa_par[0]]
        pa_par[0] ^= 1
        trev = None
        for c in range(4):
            w = [e5, ev_idb]
            if c == 0:
                w.append(ps_free.get(pid))
            trev = p.tr(psap(slotA, c * 128, [128, [1, 128]]),
                        AP(evb, 1024 + c * 128, [[2048, 128], [1, 128]]),
                        ident_ap, inc=(c == 3), waits=w,
                        start=(c == 0), stop=(c == 3))
        ehi = p.v_copy(sb(ev2, 0, 128, 0, [[1, 512]]),
                       psap(slotA, 0, [128, [1, 512]]),
                       waits=[trev, buf_free.get(("gather",))])
        elo = p.v_tt(ALU.subtract, sb(ev2, 0, 128, 512, [[1, 512]]),
                     psap(slotA, 0, [128, [1, 512]]),
                     sb(ev2, 0, 128, 0, [[1, 512]]))
        ps_free[pid] = elo
        buf_free[("ev",)] = elo
        d1 = p.dout(scat_hi, AP(ev2, 0, [[2048, 128], [128, 4], [1, 128]]),
                    waits=[elo])
        d2 = p.dout(scat_lo, AP(ev2, 512, [[2048, 128], [128, 4], [1, 128]]),
                    waits=[elo])
        buf_free[("gather",)] = d2
        return d2

    def mm3(slot, lhs_h, lhs_coff, lhs_lo_off, rhs_h, rhs_coff, rhs_lo_off,
            nk, kw, ow, first_waits, extra_mms=None):
        mm_ev = None
        nmm = nk * len(PASSES) + (len(extra_mms) if extra_mms else 0)
        idx = 0
        for c in range(nk):
            for (a, b) in PASSES:
                w = first_waits if idx == 0 else []
                last = (idx == nmm - 1)
                mm_ev = p.mm(psap(slot, 0, [128, [1, ow]]),
                             sb(lhs_h, 0, 128, a * lhs_lo_off + lhs_coff + c * kw, [[1, kw]]),
                             sb(rhs_h, 0, 128, b * rhs_lo_off + rhs_coff + c * ow, [[1, ow]]),
                             start=(idx == 0), stop=last, inc=last, waits=w)
                idx += 1
        return mm_ev

    ht_ev = ln_phase(x_d, ht_s, None)

    qt_write_evs = []
    kvT_ready = []
    slab_free = None
    last_mm = None
    buf_free[("slab",)] = None
    for sj in range(9):
        if sj == 0:
            spl = load_slab_split(wc_d, DL, 0, ND, DL, 0, [slab_free])
        else:
            spl = load_slab_split(wq_d, D, (sj - 1) * OS, ND, OS, 0, [slab_free])
        for i in range(NT):
            par = i % 2
            htev = p.din(sb(ht[par], 0, 128, 0, [[1, 2 * D]]),
                         AP(ht_s, i * 128 * 2 * D, [[2 * D, 128], [1, 2 * D]]),
                         [buf_free.get(("ht", par)), ht_ev[-1]])
            si = (sj * NT + i) % 4
            slot = PS[si]
            mm_ev = None
            idx = 0
            for c in range(ND):
                for (a, b) in PASSES:
                    w = ([htev] + spl + [ps_free.get(("PS", si))]) if idx == 0 else []
                    last = (idx == ND * 3 - 1)
                    mm_ev = p.mm(psap(slot, 0, [128, [1, OS]]),
                                 sb(ht[par], 0, 128, a * D + c * 128, [[1, 128]]),
                                 sb(slab, 0, 128, b * 16384 + c * OS, [[1, OS]]),
                                 start=(idx == 0), stop=last, inc=last, waits=w)
                    idx += 1
            last_mm = mm_ev
            buf_free[("ht", par)] = mm_ev
            if sj == 0:
                eg0 = p.v_copy(AP(evb, 1536, [[2048, 128], [1, 512]]),
                               psap(slot, 0, [128, [1, 512]]),
                               waits=[mm_ev, buf_free.get(("gather",))])
                ps_free[("PS", si)] = eg0
                pid = ("PA", pa_par[0])
                slotA = PA[pa_par[0]]
                pa_par[0] ^= 1
                trev = None
                for c in range(4):
                    w = [eg0, ev_idb]
                    if c == 0:
                        w.append(ps_free.get(pid))
                    trev = p.tr(psap(slotA, c * 128, [128, [1, 128]]),
                                AP(evb, 1536 + c * 128, [[2048, 128], [1, 128]]),
                                ident_ap, inc=(c == 3), waits=w,
                                start=(c == 0), stop=(c == 3))
                buf_free[("gather",)] = trev
                ekh = p.v_copy(AP(kvt, i * 128, [[16384, 128], [2048, 4], [1, 128]]),
                               AP(slotA[0], slotA[1],
                                  [[slotA[0].shape[1], 128], [128, 4], [1, 128]]),
                               waits=[trev])
                ekv = p.v_tt(ALU.subtract,
                             AP(kvt, 8192 + i * 128, [[16384, 128], [2048, 4], [1, 128]]),
                             AP(slotA[0], slotA[1],
                                [[slotA[0].shape[1], 128], [128, 4], [1, 128]]),
                             AP(kvt, i * 128, [[16384, 128], [2048, 4], [1, 128]]))
                ps_free[pid] = ekv
                kvT_ready.append(ekv)
            else:
                hb = 4 * (sj - 1)
                sc_hi = AP(qt_s, hb * 128 * 2 * S + i * 128,
                           [[2 * S, 128], [128 * 2 * S, 4], [1, 128]])
                sc_lo = AP(qt_s, hb * 128 * 2 * S + S + i * 128,
                           [[2 * S, 128], [128 * 2 * S, 4], [1, 128]])
                qt_write_evs.append(rope_evac(slot, ("PS", si), i, sc_hi, sc_lo, mm_ev))
        slab_free = last_mm

    kt_write_evs = []
    v_write_evs = []
    for sj in range(16):
        isk = sj < 8
        j = sj if isk else sj - 8
        wsrc_d = wk_d if isk else wv_d
        spl = load_slab_split(wsrc_d, D, j * OS, NL, OS, 0, [slab_free])
        for i in range(NT):
            si = i % 4
            slot = PS[si]
            mm_ev = None
            idx = 0
            for c in range(NL):
                for (a, b) in PASSES:
                    w = (spl + kvT_ready + [ps_free.get(("PS", si))]) if idx == 0 else []
                    last = (idx == NL * 3 - 1)
                    mm_ev = p.mm(psap(slot, 0, [128, [1, OS]]),
                                 AP(kvt, a * 8192 + c * 2048 + i * 128, [[16384, 128], [1, 128]]),
                                 sb(slab, 0, 128, b * 16384 + c * OS, [[1, OS]]),
                                 start=(idx == 0), stop=last, inc=last, waits=w)
                    idx += 1
            last_mm = mm_ev
            if isk:
                hb = 4 * j
                sc_hi = AP(kt_s, hb * 128 * 2 * S + i * 128,
                           [[2 * S, 128], [128 * 2 * S, 4], [1, 128]])
                sc_lo = AP(kt_s, hb * 128 * 2 * S + S + i * 128,
                           [[2 * S, 128], [128 * 2 * S, 4], [1, 128]])
                kt_write_evs.append(rope_evac(slot, ("PS", si), i, sc_hi, sc_lo, mm_ev))
            else:
                sh, sl = split(psap(slot, 0, [128, [1, 512]]),
                               sb(ev2, 0, 128, 0, [[1, 512]]),
                               sb(ev2, 0, 128, 512, [[1, 512]]),
                               waits=[mm_ev, buf_free.get(("ev2a",)),
                                      buf_free.get(("gather",))])
                ps_free[("PS", si)] = sl
                d1 = p.dout(AP(v_s, i * 128 * 2 * D + j * OS, [[2 * D, 128], [1, OS]]),
                            sb(ev2, 0, 128, 0, [[1, 512]]), waits=[sl])
                d2 = p.dout(AP(v_s, i * 128 * 2 * D + D + j * OS, [[2 * D, 128], [1, OS]]),
                            sb(ev2, 0, 128, 512, [[1, 512]]), waits=[sl])
                buf_free[("ev2a",)] = d2
                v_write_evs.append(d2)
        slab_free = last_mm

    phaseA_end = [qt_write_evs[-1], kt_write_evs[-1], v_write_evs[-1]]
    a3_t = p.now("tensor", "st")
    a3_v = p.now("vector", "sv")
    buf_free[("bq", 0)] = buf_free[("bq", 1)] = a3_t
    ps_free[("psT", 0)] = ps_free[("psT", 1)] = a3_t
    buf_free[("zrep",)] = buf_free[("atev",)] = a3_v
    buf_free[("zrow",)] = a3_v
    buf_free[("psTf", 0)] = buf_free[("psTf", 1)] = a3_v

    at_write_evs = []
    qkv_ev = {}
    for h in range(NH + 1):
        if h < NH:
            ph = h % 2
            base = ph * 16384
            w = [buf_free.get(("bq", ph))] + (phaseA_end if h < 2 else [])
            e_q = p.din(AP(slab, base, [[32768, 128], [1, 2 * S]]),
                        AP(qt_s, h * 128 * 2 * S, [[2 * S, 128], [1, 2 * S]]),
                        w, fence=False)
            e_k = p.din(AP(slab, base + 4096, [[32768, 128], [1, 2 * S]]),
                        AP(kt_s, h * 128 * 2 * S, [[2 * S, 128], [1, 2 * S]]),
                        w, fence=False)
            e_v = []
            for comp in range(2):
                for k in range(2):
                    e_v.append(p.din(
                        AP(slab, base + 8192 + comp * 2048 + k * 1024,
                           [[32768, 128], [128, 8], [1, 128]]),
                        AP(v_s, comp * D + h * 128 + k * 8 * 128 * 2 * D,
                           [[2 * D, 128], [128 * 2 * D, 8], [1, 128]]), w,
                        fence=(comp == 1 and k == 1)))
            qkv_ev[h] = [e_v[-1]]
        if h == 0:
            continue
        hh = h - 1
        ph = hh % 2
        base = ph * 16384
        for g in range(2):
            last_z = None
            sl_of = {}

            def emit_qk(j):
                pj = j % 2
                pbS = pbS0 if pj == 0 else pbS1
                wqk = qkv_ev[hh] + [ps_free.get(("BS", pj))]
                idx = 0
                for half in range(2):
                    for (a, b) in PASSES:
                        w = wqk if idx == 0 else []
                        p.mm(AP(pbS, half * 512, [[1024, 128], [1, 512]]),
                             AP(slab, base + 4096 + a * 2048 + j * 128,
                                [[32768, 128], [1, 128]]),
                             AP(slab, base + b * 2048 + g * 1024 + half * 512,
                                [[32768, 128], [1, 512]]),
                             start=(idx % 3 == 0), stop=(idx % 3 == 2),
                             inc=(idx == 5), waits=w)
                        idx += 1
                e_qk = p.now("tensor", "st")
                e_ex = p.s_act(AP(evb, pj * 1024, [[2048, 128], [1, 1024]]),
                               AP(pbS, 0, [[1024, 128], [1, 1024]]),
                               AF.Exp, scale=cQKS,
                               waits=[e_qk, ev_cst, buf_free.get(("psTf", pj))])
                ps_free[("BS", pj)] = e_ex
                sh, sl = split(AP(evb, pj * 1024, [[2048, 128], [1, 1024]]),
                               AP(kvt, pj * 2048, [[16384, 128], [1, 1024]]),
                               AP(kvt, pj * 2048 + 1024, [[16384, 128], [1, 1024]]),
                               waits=[e_ex, ps_free.get(("psT", pj))])
                buf_free[("psTf", pj)] = sl
                sl_of[j] = sl

            def emit_pvz(j):
                nonlocal last_z
                pj = j % 2
                psTb = pj * 2048
                sl = sl_of[j]
                idx = 0
                for half in range(2):
                    for (a, b) in PASSES:
                        w = [sl]
                        if j == 0 and idx == 0:
                            w += [ps_free.get(("pbA",))]
                        p.mm(AP(pbA, half * 512, [[1024, 128], [1, 512]]),
                             AP(slab, base + 8192 + a * 2048 + j * 128,
                                [[32768, 128], [1, 128]]),
                             AP(kvt, psTb + b * 1024 + half * 512,
                                [[16384, 128], [1, 512]]),
                             start=(j == 0 and idx % 3 == 0),
                             stop=(j == 15 and idx % 3 == 2),
                             inc=False, waits=w)
                        idx += 1
                for half in range(2):
                    for comp in range(2):
                        w = []
                        if j == 0 and half == 0 and comp == 0:
                            w = [ev_onb, ps_free.get(("pbZ",))]
                        last_z = p.mm(AP(pbZ, half * 512, [[1024, 1], [1, 512]]),
                                      onesb_ap,
                                      AP(kvt, psTb + comp * 1024 + half * 512,
                                         [[16384, 128], [1, 512]]),
                                      start=(j == 0 and comp == 0),
                                      stop=(j == 15 and comp == 1),
                                      inc=(half == 1 and comp == 1), waits=w)
                ps_free[("psT", pj)] = last_z

            for j in range(17):
                if j < 16:
                    emit_qk(j)
                if j >= 1:
                    emit_pvz(j - 1)
            e_zc = p.v_copy(AP(xa[0], 1024, [[4096, 1], [1, 1024]]),
                            AP(pbZ, 0, [[1024, 1], [1, 1024]]),
                            waits=[last_z, buf_free.get(("zrow",))])
            ps_free[("pbZ",)] = e_zc
            e_zr = p.v_recip(AP(xa[0], 2048, [[4096, 1], [1, 1024]]),
                             AP(xa[0], 1024, [[4096, 1], [1, 1024]]))
            bc_ev = None
            for half in range(2):
                w = ([e_zr, ev_onr, ps_free.get(("BS", 0))] if half == 0 else [])
                bc_ev = p.mm(AP(pbS0, half * 512, [[1024, 128], [1, 512]]),
                             onesr_ap,
                             AP(xa[0], 2048 + half * 512, [[4096, 1], [1, 512]]),
                             start=True, stop=True, inc=(half == 1), waits=w)
            e_zv = p.v_copy(AP(evb, 1024, [[2048, 128], [1, 1024]]),
                            AP(pbS0, 0, [[1024, 128], [1, 1024]]),
                            waits=[bc_ev, buf_free.get(("zrep",))])
            ps_free[("BS", 0)] = e_zv
            e_at = p.v_tt(ALU.mult, AP(xa[0], 0, [[4096, 128], [1, 1024]]),
                          AP(pbA, 0, [[1024, 128], [1, 1024]]),
                          AP(evb, 1024, [[2048, 128], [1, 1024]]),
                          waits=[e_zv, buf_free.get(("atev",)), last_z])
            ps_free[("pbA",)] = e_at
            buf_free[("zrep",)] = e_at
            buf_free[("psTf", 1)] = e_at
            buf_free[("zrow",)] = e_zr
            sh, sl = split(AP(xa[0], 0, [[4096, 128], [1, 1024]]),
                           sb(ev2, 0, 128, 0, [[1, 1024]]),
                           sb(ev2, 0, 128, 1024, [[1, 1024]]),
                           waits=[e_at, buf_free.get(("ev2a",))])
            buf_free[("atev",)] = sl
            d1 = p.dout(AP(at_s, hh * 128 * 2 * S + g * 1024, [[2 * S, 128], [1, 1024]]),
                        sb(ev2, 0, 128, 0, [[1, 1024]]), waits=[sl])
            d2 = p.dout(AP(at_s, hh * 128 * 2 * S + S + g * 1024, [[2 * S, 128], [1, 1024]]),
                        sb(ev2, 0, 128, 1024, [[1, 1024]]), waits=[sl])
            buf_free[("ev2a",)] = d2
            at_write_evs.append(d2)
        buf_free[("bq", ph)] = p.now("tensor", "st")

    b_end_v = p.now("vector", "sv")
    b_end_out = p.now("gpsimd", "sout")
    buf_free[("xsl", 0)] = buf_free[("xsl", 1)] = b_end_v
    buf_free[("eo",)] = b_end_v
    buf_free[("ev",)] = b_end_v
    buf_free[("gather",)] = b_end_v
    buf_free[("xa", 0)] = buf_free[("xa", 1)] = b_end_v
    buf_free[("slab",)] = p.now("tensor", "st")

    o1_tile_ev = {}
    slab_free = p.now("tensor", "st")
    for sj in range(8):
        spl = load_slab_split(wo_d, D, sj * OS, ND, OS, 0, [slab_free])
        for i in range(NT):
            par = i % 2
            w = [buf_free.get(("ht", par))] + (at_write_evs if (sj == 0 and i == 0) else [])
            a_evs = []
            for comp in range(2):
                for k in range(4):
                    a_evs.append(p.din(
                        sb(ht[par], 0, 128, comp * D + k * 1024, [[128, 8], [1, 128]]),
                        AP(at_s, comp * S + (8 * k) * 128 * 2 * S + i * 128,
                           [[2 * S, 128], [128 * 2 * S, 8], [1, 128]]), w,
                        fence=(comp == 1 and k == 3)))
            a_evs = [a_evs[-1]]
            exs = p.din(AP(evb, par * 512, [[2048, 128], [1, 512]]),
                        AP(x_d, i * 128 * D + sj * OS, [[D, 128], [1, OS]]),
                        [buf_free.get(("xsl", par))])
            si = i % 4
            slot = PS[si]
            mm_ev = None
            idx = 0
            for c in range(ND):
                for (a, b) in PASSES:
                    w = (a_evs + spl + [ps_free.get(("PS", si))]) if idx == 0 else []
                    last = (idx == ND * 3 - 1)
                    mm_ev = p.mm(psap(slot, 0, [128, [1, OS]]),
                                 sb(ht[par], 0, 128, a * D + c * 128, [[1, 128]]),
                                 sb(slab, 0, 128, b * 16384 + c * OS, [[1, OS]]),
                                 start=(idx == 0), stop=last, inc=last, waits=w)
                    idx += 1
            last_mm = mm_ev
            buf_free[("ht", par)] = mm_ev
            e_o1 = p.v_tt(ALU.add, AP(evb, 1024, [[2048, 128], [1, 512]]),
                          psap(slot, 0, [128, [1, 512]]),
                          AP(evb, par * 512, [[2048, 128], [1, 512]]),
                          waits=[mm_ev, exs, buf_free.get(("eo",))])
            ps_free[("PS", si)] = e_o1
            buf_free[("xsl", par)] = e_o1
            dev = p.dout(AP(o1_s, i * 128 * D + sj * OS, [[D, 128], [1, OS]]),
                         AP(evb, 1024, [[2048, 128], [1, 512]]), waits=[e_o1])
            buf_free[("eo",)] = dev
            o1_tile_ev[(sj, i)] = dev
        slab_free = last_mm

    c1_last = o1_tile_ev[(7, NT - 1)]

    lt_ev = ln_phase(o1_s, lt_s, c1_last)

    spl_a = load_slab_split(a2t_d, 128, 0, ND, 128, 0, [slab_free])
    spl_r = load_slab_split(rwt_d, 8, 0, ND, 8, 4096, [slab_free])
    t_evs = []
    rl_evs = []
    mm_r = None
    for i in range(NT):
        par = i % 2
        htev = p.din(sb(ht[par], 0, 128, 0, [[1, 2 * D]]),
                     AP(lt_s, i * 128 * 2 * D, [[2 * D, 128], [1, 2 * D]]),
                     [buf_free.get(("ht", par)), lt_ev[-1]])
        mm_t = None
        idx = 0
        for c in range(ND):
            for (a, b) in PASSES:
                w = ([htev] + spl_a + spl_r + [ps_free.get(("PS", 0))]) if idx == 0 else []
                w2 = [ps_free.get(("PS", 2))] if idx == 0 else []
                last = (idx == ND * 3 - 1)
                mm_t = p.mm(AP(pbS0, 0, [[1024, 128], [1, 128]]),
                            sb(ht[par], 0, 128, a * D + c * 128, [[1, 128]]),
                            sb(slab, 0, 128, b * 16384 + c * 128, [[1, 128]]),
                            start=(idx == 0), stop=last, inc=last, waits=w)
                mm_r = p.mm(AP(pbS1, 0, [[1024, 128], [1, 8]]),
                            sb(ht[par], 0, 128, a * D + c * 128, [[1, 128]]),
                            sb(slab, 0, 128, b * 16384 + 4096 + c * 8, [[1, 8]]),
                            start=(idx == 0), stop=last, inc=last, waits=w2)
                idx += 1
        buf_free[("ht", par)] = mm_r
        e_t = p.v_copy(AP(hbuf, i * 128, [[4096, 128], [1, 128]]),
                       AP(pbS0, 0, [[1024, 128], [1, 128]]), waits=[mm_t])
        ps_free[("PS", 0)] = e_t
        e_rl = p.v_copy(AP(evb, i * 8, [[2048, 128], [1, 8]]),
                        AP(pbS1, 0, [[1024, 128], [1, 8]]), waits=[mm_r])
        ps_free[("PS", 2)] = e_rl
        t_evs.append(e_t)
        rl_evs.append(e_rl)
    lg_mm = None
    for i in range(NT):
        w = [rl_evs[i], ev_msk] + ([ps_free.get(("pbZ",))] if i == 0 else [])
        lg_mm = p.mm(AP(pbZ, 0, [[1024, 1], [1, 8]]),
                     sb(maskc, 0, 128, i, [[1, 1]]),
                     AP(evb, i * 8, [[2048, 128], [1, 8]]),
                     start=(i == 0), stop=(i == NT - 1),
                     inc=(i == NT - 1), waits=w)
    lg = AP(smt, 0, [[64, 1], [1, 8]])
    mx = AP(smt, 8, [[64, 1], [1, 1]])
    nmx = AP(smt, 9, [[64, 1], [1, 1]])
    pr = AP(smt, 16, [[64, 1], [1, 8]])
    sm1 = AP(smt, 24, [[64, 1], [1, 1]])
    rsm = AP(smt, 25, [[64, 1], [1, 1]])
    rout = AP(smt, 32, [[64, 1], [1, 8]])
    e1 = p.v_tt(ALU.add, lg, AP(pbZ, 0, [[1024, 1], [1, 8]]),
                AP(routerb, 0, [[8, 1], [1, 8]]), waits=[lg_mm, ev_rb])
    ps_free[("pbZ",)] = e1
    p.v_red(ALU.max, mx, lg)
    p.v_ts(nmx, mx, -1.0, None, ALU.mult, ALU.bypass)
    e2 = p.s_act(pr, lg, AF.Exp, bias=nmx, scale=1.0, accum=sm1,
                 waits=[p.now("vector", "sv")])
    e3r = p.v_recip(rsm, sm1, waits=[e2])
    e4 = p.v_ts(rout, pr, rsm, 2.0, ALU.mult, ALU.mult, waits=[e2],
                fwaits=[e3r])
    e5 = p.mm(AP(pbS0, 0, [[1024, 128], [1, 8]]), onesr_ap, rout,
              start=True, stop=True, inc=True,
              waits=[e4, ev_onr, ps_free.get(("PS", 0))])
    e6 = p.v_copy(AP(st2, 0, [[8, 128], [1, 8]]),
                  AP(pbS0, 0, [[1024, 128], [1, 8]]), waits=[e5])
    ps_free[("PS", 0)] = e6
    e7 = None
    for ee in range(8):
        e7 = p.v_ts(AP(xa[1], ee * 16, [[4096, 128], [128, 16], [1, 16]]),
                    AP(hbuf, ee * 16, [[4096, 128], [128, 16], [1, 16]]),
                    sb(st2, 0, 128, ee, [[1, 1]]), None, ALU.mult, ALU.bypass,
                    waits=t_evs + [buf_free.get(("xa", 1))] if ee == 0 else [],
                    fwaits=[e6] if ee == 0 else [])
    ts_evs = []
    for i in range(NT):
        pid = ("PA", pa_par[0])
        slotA = PA[pa_par[0]]
        pa_par[0] ^= 1
        trev = p.tr(psap(slotA, 0, [128, [1, 128]]),
                    AP(xa[1], i * 128, [[4096, 128], [1, 128]]),
                    ident_ap, inc=True,
                    waits=[e7, ev_idb, ps_free.get(pid)],
                    start=True, stop=True)
        e8h = p.v_copy(AP(kvt, i * 128, [[16384, 128], [1, 128]]),
                       psap(slotA, 0, [128, [1, 128]]), waits=[trev])
        e8 = p.v_tt(ALU.subtract,
                    AP(kvt, 2048 + i * 128, [[16384, 128], [1, 128]]),
                    psap(slotA, 0, [128, [1, 128]]),
                    AP(kvt, i * 128, [[16384, 128], [1, 128]]))
        ps_free[pid] = e8
        ts_evs.append(e8)
    slab_free = mm_r
    c2b_t = p.now("tensor", "st")
    buf_free[("brep",)] = buf_free[("tmp",)] = buf_free[("eo2",)] = c2b_t
    buf_free[("xo1", 0)] = buf_free[("xo1", 1)] = c2b_t

    exv = p.din(sb(xa[0], 0, 128, 0, [[1, D]]), AP(b2_d, 0, [[D, 128], [1, D]]),
                ts_evs + [b_end_out, buf_free.get(("xa", 0))])
    eb2h = p.v_copy(AP(kvt, 4096, [[16384, 128], [1, D]]),
                    sb(xa[0], 0, 128, 0, [[1, D]]), waits=[exv])
    eb2l = p.v_tt(ALU.subtract, AP(kvt, 8192, [[16384, 128], [1, D]]),
                  sb(xa[0], 0, 128, 0, [[1, D]]),
                  AP(kvt, 4096, [[16384, 128], [1, D]]))
    buf_free[("xa", 0)] = eb2l
    out_evs = []
    e_t1 = None
    for sj in range(8):
        spl = load_slab_split(wf_d, D, sj * OS, ND, OS, 0, [slab_free])
        e_br = p.din(AP(hbuf, 2048, [[D, 128], [1, 512]]),
                     AP(brep_d, sj * OS, [[D, 128], [1, OS]]),
                     [buf_free.get(("brep",))])
        for i in range(NT):
            par = i % 2
            htev = p.din(sb(ht[par], 0, 128, 0, [[1, 2 * D]]),
                         AP(lt_s, i * 128 * 2 * D, [[2 * D, 128], [1, 2 * D]]),
                         [buf_free.get(("ht", par)), lt_ev[-1]])
            exs = p.din(AP(evb, par * 512, [[2048, 128], [1, 512]]),
                        AP(o1_s, i * 128 * D + sj * OS, [[D, 128], [1, OS]]),
                        [buf_free.get(("xo1", par)), o1_tile_ev[(sj, i)]])
            si = i % 4
            slot = PS[si]
            mm_ev = None
            idx = 0
            for c in range(ND):
                for (a, b) in PASSES:
                    w = ([htev] + spl + [ps_free.get(("PS", si))]) if idx == 0 else []
                    mm_ev = p.mm(psap(slot, 0, [128, [1, OS]]),
                                 sb(ht[par], 0, 128, a * D + c * 128, [[1, 128]]),
                                 sb(slab, 0, 128, b * 16384 + c * OS, [[1, OS]]),
                                 start=(idx == 0), stop=False, inc=False, waits=w)
                    idx += 1
            for pi, (a, b) in enumerate(PASSES):
                mm_ev = p.mm(psap(slot, 0, [128, [1, OS]]),
                             AP(kvt, a * 2048 + i * 128, [[16384, 128], [1, 128]]),
                             AP(kvt, 4096 + b * 4096 + sj * OS, [[16384, 128], [1, OS]]),
                             start=False, stop=(pi == 2), inc=(pi == 2),
                             waits=[eb2l] if pi == 0 else [])
            last_mm = mm_ev
            buf_free[("ht", par)] = mm_ev
            e_t1 = p.v_tt(ALU.add, AP(hbuf, 2560, [[D, 128], [1, 512]]),
                          psap(slot, 0, [128, [1, 512]]),
                          AP(hbuf, 2048, [[D, 128], [1, 512]]),
                          waits=[mm_ev, e_br, buf_free.get(("tmp",))])
            ps_free[("PS", si)] = e_t1
            e_t2 = p.v_tt(ALU.add, AP(hbuf, 3072, [[D, 128], [1, 512]]),
                          AP(hbuf, 2560, [[D, 128], [1, 512]]),
                          AP(evb, par * 512, [[2048, 128], [1, 512]]),
                          waits=[exs, buf_free.get(("eo2",))])
            buf_free[("tmp",)] = e_t2
            buf_free[("xo1", par)] = e_t2
            dev = p.dout(AP(out_d, i * 128 * D + sj * OS, [[D, 128], [1, OS]]),
                         AP(hbuf, 3072, [[D, 128], [1, 512]]), waits=[e_t2])
            buf_free[("eo2",)] = dev
            out_evs.append(dev)
        slab_free = last_mm
        buf_free[("brep",)] = e_t1

    p.ops["gpsimd"].append(("w", "sout", p.cnt["sout"]))

    with (
        nc.semaphore("sin") as h_sin,
        nc.semaphore("sout") as h_sout,
        nc.semaphore("st") as h_st,
        nc.semaphore("sv") as h_sv,
        nc.semaphore("ss") as h_ss,
        nc.Block() as block,
    ):
        sems = {"sin": h_sin, "sout": h_sout, "st": h_st, "sv": h_sv, "ss": h_ss}

        def mkbody(eng_ops):
            def body(eng):
                for op in eng_ops:
                    if op[0] == "w":
                        eng.wait_ge(sems[op[1]], op[2])
                    else:
                        inst = op[1](eng)
                        for (s, k) in op[2]:
                            inst.then_inc(sems[s], k)
            return body

        block.tensor(mkbody(p.ops["tensor"]))
        block.vector(mkbody(p.ops["vector"]))
        block.scalar(mkbody(p.ops["scalar"]))
        block.sync(mkbody(p.ops["sync"]))
        block.gpsimd(mkbody(p.ops["gpsimd"]))

    return nc


def _host_inputs(inputs):
    f32 = np.float32
    Wq = np.ascontiguousarray(inputs["L0_attn_Wq"], f32)
    Wc = np.ascontiguousarray(inputs["L0_attn_Wc"], f32)
    Wk = np.ascontiguousarray(inputs["L0_attn_Wk"], f32)
    Wv = np.ascontiguousarray(inputs["L0_attn_Wv"], f32)
    Wo = np.ascontiguousarray(inputs["L0_attn_Wo"], f32)
    Wf = np.ascontiguousarray(np.asarray(inputs["L0_ffn_W"]).T, f32)
    A = np.asarray(inputs["L0_ffn_A"], f32)
    B = np.asarray(inputs["L0_ffn_B"], f32)
    A2T = np.ascontiguousarray(A.reshape(128, D).T, f32)
    B2 = np.ascontiguousarray(np.transpose(B, (0, 2, 1)).reshape(128, D), f32)
    RWT = np.ascontiguousarray(np.asarray(inputs["L0_ffn_router_W"]).T, f32)
    brep = np.ascontiguousarray(
        np.broadcast_to(np.asarray(inputs["L0_ffn_b"], f32), (128, D)), f32)
    routerb = np.ascontiguousarray(
        np.asarray(inputs["L0_ffn_router_b"], f32).reshape(1, 8), f32)

    pos = np.arange(S, dtype=np.float64)
    inv = 1.0 / (10000.0 ** (np.arange(0, 128, 2, dtype=np.float64) / 128.0))
    ang = pos[:, None] * inv[None, :]
    ang = np.concatenate([ang, ang], axis=1)
    cos = np.cos(ang).astype(f32)
    sin = np.sin(ang).astype(f32)
    costok = np.zeros((128, S), f32)
    sintok = np.zeros((128, S), f32)
    for i in range(NT):
        costok[:, i * 128:(i + 1) * 128] = cos[i * 128:(i + 1) * 128, :]
        sintok[:, i * 128:(i + 1) * 128] = sin[i * 128:(i + 1) * 128, :]

    maskc = np.zeros((128, NT), f32)
    for t in range(611, 2047):
        maskc[t % 128, t // 128] = np.float32(np.float32(1.0) / np.float32(1436.0))

    import ml_dtypes
    identf = np.eye(128, dtype=f32)
    onesb = np.ones((128, 1), ml_dtypes.bfloat16)
    onesr = np.ones((1, 128), f32)
    consts = np.zeros((128, 4), f32)
    consts[:, 0] = NEPS
    consts[:, 1] = -0.5
    consts[:, 2] = QK_SCALE

    return dict(wq=Wq, wc=Wc, wk=Wk, wv=Wv, wo=Wo, wf=Wf, a2t=A2T, b2=B2,
                rwt=RWT, brep=brep, routerb=routerb, costok=costok,
                sintok=sintok, maskc=maskc, identf=identf, onesb=onesb,
                onesr=onesr, consts=consts)


def kernel(**inputs):
    xfull = np.ascontiguousarray(inputs["L0_attn_hidden_states"], np.float32)
    com = _host_inputs(inputs)
    nc = build_nc()
    in_maps = [dict(com, x=xfull[b]) for b in range(8)]
    res = run_bass_kernel_spmd(nc, in_maps, core_ids=list(range(8)))
    return np.stack([res.results[b]["out"] for b in range(8)], axis=0)
